# version 17
# baseline (speedup 1.0000x reference)
import numpy as np
import ml_dtypes
from contextlib import ExitStack

import concourse.bass as bass
import concourse.mybir as mybir
from concourse.bass_utils import run_bass_kernel_spmd

BF16 = mybir.dt.bfloat16
F32 = mybir.dt.float32
AF = mybir.ActivationFunctionType
ALU = mybir.AluOpType
AXX = mybir.AxisListType.X
NPBF = ml_dtypes.bfloat16

H, DK, DV, D = 4, 256, 256, 1024
INNER, DHID = 4096, 2048
C = 128
ENGS = ("tensor", "vector", "scalar", "gpsimd", "sync")
DSEM = {"sync": "dS", "scalar": "dA", "gpsimd": "dG"}


def host_consts():
    i128 = np.eye(128, dtype=np.float32)
    strictL = np.tril(np.ones((128, 128), np.float32), -1)
    bd32 = np.zeros((128, 128), np.float32)
    for g in range(4):
        bd32[g*32:(g+1)*32, g*32:(g+1)*32] = 1.0
    mC32 = np.zeros((128, 128), np.float32)
    for g0 in range(0, 128, 64):
        mC32[g0:g0+32, g0+32:g0+64] = 1.0
    mC64 = np.zeros((128, 128), np.float32)
    mC64[0:64, 64:128] = 1.0
    return {
        "ident": i128,
        "mT_bd": -(strictL * bd32),
        "mTt": -strictL.T,
        "mTt_bd": -(strictL.T * bd32),
        "mloc": np.triu(np.ones((128, 128), np.float32), 0),
        "mC32": mC32, "mC64": mC64,
    }


class Buf:
    __slots__ = ("name", "last_w", "readers", "war", "arena", "off", "n")
    def __init__(self, name, war=()):
        self.name = name
        self.last_w = None
        self.readers = []
        self.war = list(war)


class Plan:
    def __init__(self):
        self.items = {e: [] for e in ENGS}
        self.seq = {e: 0 for e in ENGS}
        self.drain_upto = {e: -1 for e in ENGS}
        self.waited = {e: {} for e in ENGS}
        self.dma_nslots = {"dS": 12, "dA": 12, "dG": 6}
        self.dma_issue = {q: 0 for q in self.dma_nslots}
        self.dma_ord = {}
        self.dma_slot_buf = {}
        self.marks = {e: set() for e in ENGS}

    def op(self, eng, emit, reads=(), writes=(), dma=None, tag=""):
        semname = None
        if dma is not None:
            slot = self.dma_issue[dma] % self.dma_nslots[dma]
            self.dma_issue[dma] += 1
            semname = f"{dma}{slot}"
            key = (dma, slot)
            if key not in self.dma_slot_buf:
                self.dma_slot_buf[key] = Buf("dmaslot." + semname)
            writes = list(writes) + [self.dma_slot_buf[key]]
        deps = []
        for b in reads:
            if b.last_w is not None:
                deps.append(b.last_w)
        for b in writes:
            if b.last_w is not None:
                deps.append(b.last_w)
            deps.extend(b.readers)
            deps.extend(b.war)
        item = {"emit": emit, "waits": [], "drain": False, "inc": False,
                "dma": dma, "tag": tag}
        myseq = self.seq[eng]
        need_drain = False
        for d in set(deps):
            if d[0] == "eng":
                _, de, ds = d
                if de == eng:
                    if eng != "tensor" and ds > self.drain_upto[eng]:
                        need_drain = True
                else:
                    if self.waited[eng].get(("eng", de), -1) < ds:
                        item["waits"].append(d)
                        self.waited[eng][("eng", de)] = ds
                        self.marks[de].add(ds)
            else:
                _, sem, val = d
                if self.waited[eng].get(("dma", sem), -1) < val:
                    item["waits"].append(d)
                    self.waited[eng][("dma", sem)] = val
        if need_drain:
            item["drain"] = True
            self.drain_upto[eng] = myseq - 1
        self.items[eng].append(item)
        self.seq[eng] += 1
        if dma is not None:
            self.dma_ord[semname] = self.dma_ord.get(semname, 0) + 16
            tok = ("dma", semname, self.dma_ord[semname])
            item["dma"] = semname
        else:
            tok = ("eng", eng, myseq)
        for b in writes:
            b.last_w = tok
            b.readers = []
            b.war = []
        for b in reads:
            b.readers.append(tok)
        return tok

    def finalize(self):
        self.rank = {}
        for e in ENGS:
            marked = sorted(self.marks[e])
            self.rank[e] = {sq: i + 1 for i, sq in enumerate(marked)}
            for sq in marked:
                self.items[e][sq]["inc"] = True

    def wait_value(self, tok):
        if tok[0] == "dma":
            return tok[2]
        return self.rank[tok[1]][tok[2]]


class Arena:
    def __init__(self, tensor, cols, name):
        self.t = tensor
        self.cols = cols
        self.name = name
        self.freelist = [(0, cols)]
        self.dead = []
        self.peak = 0
        self.used = 0

    def alloc(self, name, n):
        for i, (off, ln) in enumerate(self.freelist):
            if ln >= n:
                if ln == n:
                    self.freelist.pop(i)
                else:
                    self.freelist[i] = (off + n, ln - n)
                war, keep = [], []
                for (doff, dlen, toks) in self.dead:
                    if doff < off + n and off < doff + dlen:
                        war.extend(toks)
                        if doff < off:
                            keep.append((doff, off - doff, toks))
                        if doff + dlen > off + n:
                            keep.append((off + n, doff + dlen - (off + n), toks))
                    else:
                        keep.append((doff, dlen, toks))
                self.dead = keep
                b = Buf(f"{self.name}.{name}", war=war)
                b.arena = self
                b.off = off
                b.n = n
                self.used += n
                self.peak = max(self.peak, self.used)
                return b
        raise MemoryError(f"arena {self.name}: alloc {n} for {name} failed "
                          f"(used {self.used}/{self.cols})")

    def free(self, b):
        toks = list(b.readers)
        if b.last_w is not None:
            toks.append(b.last_w)
        self.dead.append((b.off, b.n, toks))
        self.used -= b.n
        self.freelist.append((b.off, b.n))
        self.freelist.sort()
        merged = []
        for off, ln in self.freelist:
            if merged and merged[-1][0] + merged[-1][1] == off:
                merged[-1] = (merged[-1][0], merged[-1][1] + ln)
            else:
                merged.append((off, ln))
        self.freelist = merged


def bsl(b, c0, c1, p0=0, p1=128):
    return b.arena.t[p0:p1, b.off + c0:b.off + c1]


def bap(b, col, dims, p0=0, pn=128):
    w = b.arena.t.shape[1]
    return bass.AP(b.arena.t, p0 * w + b.off + col, [[w, pn]] + dims)


def build(TPER=512, NTILE=4):
    L = TPER * NTILE
    TT = 2 * TPER
    TCH = TT // 128
    TCS = TCH // 2
    NCH = TPER // C
    VP = TPER + 30
    QP = TPER + 6
    TGW = min(512, TT)
    NTG = TT // TGW
    WD = NCH * 128

    nc = bass.Bass("TRN2", target_bir_lowering=False)
    st = ExitStack()

    def din(name, shape, dt=F32):
        return nc.dram_tensor(name, shape, dt, kind="ExternalInput").ap()

    x_in = din("x", [2, L, D])
    w_hbm = {"q": din("W_q", [D, D]), "k": din("W_k", [D, D]), "v": din("W_v", [D, D]),
             "o": din("W_o", [D, D]), "f1": din("W_fuse1", [INNER, DHID])}
    w_b = din("W_b", [D, H])
    cw = [din("conv_w_q", [D, 4]), din("conv_w_k", [D, 4]), din("conv_w_v", [D, 4])]
    fir_s_in = din("fir_short_w", [D, 3]); fir_l_in = din("fir_long_w", [D, 31])
    b_f1 = din("b_fuse1", [DHID])
    w_f2 = din("W_fuse2", [DHID, 16]); b_f2 = din("b_fuse2", [16])
    log_tau = din("others_log_tau", [H]); norm_w_in = din("norm_w", [DV])
    ffn_hbm = {"g": din("w_gate", [INNER, D]), "u": din("w_up", [INNER, D]),
               "d": din("w_down", [D, INNER])}
    CN = host_consts()
    cst_in = {k: din("c_" + k, [128, 128], BF16) for k in CN}
    onesb_in = din("c_onesb", [1, 128], BF16)
    onesf_r = din("c_onesf_r", [1, 128], F32)
    onesf_c = din("c_onesf_c", [128, 1], F32)
    out = nc.dram_tensor("out", [2, L, D], F32, kind="ExternalOutput").ap()

    def dsc(name, shape):
        return nc.dram_tensor(name, shape, BF16).ap()
    s_w = {"q": dsc("s_wq", [D, D]), "k": dsc("s_wk", [D, D]), "v": dsc("s_wv", [D, D]),
           "o": dsc("s_wo", [D, D]), "f1": dsc("s_w1", [INNER, DHID]),
           "g": dsc("s_wgT", [D, INNER]), "u": dsc("s_wuT", [D, INNER]),
           "d": dsc("s_wdT", [INNER, D])}

    FCOLS, BCOLS = 17600, 69120
    fa_t = st.enter_context(nc.sbuf_tensor("fa", [128, FCOLS], F32))
    ba_t = st.enter_context(nc.sbuf_tensor("ba", [128, BCOLS], BF16))
    int_t = st.enter_context(nc.sbuf_tensor("ib", [128, 1024], mybir.dt.int32))
    intb = Buf("intb")
    FA = Arena(fa_t, FCOLS, "f")
    BA = Arena(ba_t, BCOLS, "b")

    ps_t, psb = {}, {}
    for nm in "ABCDEFG":
        ps_t[nm] = st.enter_context(nc.psum_tensor("ps" + nm, [128, 512], F32))
        psb[nm] = Buf("ps" + nm)
    ps_t["T"] = st.enter_context(nc.psum_tensor("psT", [128, 1024], BF16))
    psb["T"] = Buf("psT")

    sems = {}
    for e in ENGS:
        sems[e] = st.enter_context(nc.semaphore("sem_" + e))
    for q, nsl in (("dS", 12), ("dA", 12), ("dG", 6)):
        for sl in range(nsl):
            sems[f"{q}{sl}"] = st.enter_context(nc.semaphore(f"{q}{sl}"))

    P = Plan()
    dram_bufs = {}
    def dbuf(ap):
        key = ap.tensor.name
        if key not in dram_bufs:
            dram_bufs[key] = Buf("hbm." + key)
        return dram_bufs[key]

    def OP(eng, reads, writes, emit, tag=""):
        return P.op(eng, emit, reads=reads, writes=writes, tag=tag)

    def DMA(q, dst, src, reads, writes, tag="", transpose=False, slow=False):
        if transpose:
            emit = lambda e, dd=dst, ss=src: e.dma_start_transpose(dd, ss)
        elif slow:
            emit = lambda e, dd=dst, ss=src: e.dma_start(dd, ss,
                                                         allow_slow_non_contiguous=True)
        else:
            emit = lambda e, dd=dst, ss=src: e.dma_start(dd, ss)
        return P.op(q, emit, reads=reads, writes=writes, dma=DSEM[q], tag=tag)

    rotAD = ["A", "B", "C", "D"]

    S_f = FA.alloc("S", 8 * 512)
    S_b = BA.alloc("Sb", 8 * 512)
    cstb = {k: BA.alloc("c_" + k, 128) for k in CN}
    onesb = BA.alloc("onesb", 128)
    onesf = FA.alloc("onesf", 129)
    convw = FA.alloc("convw", 3 * 32)
    firw = FA.alloc("firw", 8 * 34)
    b1b = FA.alloc("b1", 16)
    b2b = FA.alloc("b2", 1)
    rtau = FA.alloc("rtau", 16)
    nwrep = BA.alloc("nwrep", 256)
    w2b = BA.alloc("w2", 16 * 16)
    wbb = BA.alloc("wb", 8 * 4)
    qtails = BA.alloc("qtails", 3 * 8 * 2 * 4)
    vtails = BA.alloc("vtails", 8 * 2 * 30)
    ffsc = FA.alloc("ffsc", 8)
    epsb = FA.alloc("eps", 3)
    x_tok = FA.alloc("x", TCH * 1024)

    def cap(name, p0=0, p1=128, c0=0, c1=128):
        return bsl(cstb[name], c0, c1, p0, p1)

    def cap_rep(name, rep):
        return bap(cstb[name], 0, [[0, rep], [1, 128]])

    bgq = []
    qtb = {}

    def pump(n):
        while n > 0 and bgq:
            bgq.pop(0)()
            n -= 1

    def pump_flush():
        while bgq:
            bgq.pop(0)()
        if qtb:
            for b in qtb["acc"]:
                FA.free(b)
            for k in ("sc1", "t5", "mb"):
                FA.free(qtb[k])
            FA.free(qtb["cb"][0]); FA.free(qtb["cb"][1])
            for k in ("qb", "qbT"):
                BA.free(qtb[k][0]); BA.free(qtb[k][1])
            qtb.clear()

    def emit_floor(srcb, src_ap, dstb, dst_ap, c1b, c1_ap, gb, g_ap, width,
                   extra_r=(), pool=False):
        ge = "gpsimd" if pool else "vector"
        OP("vector", [srcb] + list(extra_r), [intb],
           lambda e, s=src_ap, w=width: e.tensor_copy(int_t[:, 0:w], s), "flr")
        OP("vector", [intb], [c1b],
           lambda e, c=c1_ap, w=width: e.tensor_copy(c, int_t[:, 0:w]), "flr")
        OP(ge, [srcb, c1b], [gb],
           lambda e, s=src_ap, c=c1_ap, g=g_ap: e.tensor_tensor(g, s, c, ALU.subtract), "flr")
        OP("vector", [gb, c1b], [c1b],
           lambda e, c=c1_ap, g=g_ap: e.scalar_tensor_tensor(
               c, g, 1.0, c, ALU.is_ge, ALU.add), "flr")
        OP("vector", [gb], [gb],
           lambda e, g=g_ap: e.tensor_scalar(g, g, 0.0, -1.0, ALU.is_ge, ALU.add), "flr")
        OP(ge, [c1b, gb], [dstb],
           lambda e, c=c1_ap, g=g_ap, d=dst_ap: e.tensor_tensor(d, c, g, ALU.add), "flr")

    def pre_phase():
        OP("vector", [], [epsb], lambda e: e.memset(bsl(epsb, 0, 1), 1e-5), "c")
        OP("vector", [], [epsb], lambda e: e.memset(bsl(epsb, 1, 2), 1e-6), "c")
        OP("vector", [], [epsb], lambda e: e.memset(bsl(epsb, 2, 3), 0.5), "c")
        for kk in CN:
            DMA("sync", cap(kk), cst_in[kk][:, :], [dbuf(cst_in[kk])], [cstb[kk]], "c")
        DMA("sync", bsl(onesb, 0, 128, 0, 1), onesb_in[:, :], [dbuf(onesb_in)], [onesb], "c")
        DMA("sync", bsl(onesf, 0, 128, 0, 1), onesf_r[:, :], [dbuf(onesf_r)], [onesf], "c")
        DMA("sync", bsl(onesf, 128, 129), onesf_c[:, :], [dbuf(onesf_c)], [onesf], "c")
        for pi in range(3):
            DMA("sync", bap(convw, pi * 32, [[4, 8], [1, 4]]),
                bass.AP(cw[pi].tensor, 0, [[4, 128], [512, 8], [1, 4]]),
                [dbuf(cw[pi])], [convw], "c")
        DMA("sync", bap(firw, 0, [[34, 8], [1, 31]]),
            bass.AP(fir_l_in.tensor, 0, [[31, 128], [31 * 128, 8], [1, 31]]),
            [dbuf(fir_l_in)], [firw], "c")
        DMA("sync", bap(firw, 31, [[34, 8], [1, 3]]),
            bass.AP(fir_s_in.tensor, 0, [[3, 128], [3 * 128, 8], [1, 3]]),
            [dbuf(fir_s_in)], [firw], "c")
        DMA("sync", bap(b1b, 0, [[1, 16]]),
            bass.AP(b_f1.tensor, 0, [[1, 128], [128, 16]]), [dbuf(b_f1)], [b1b], "c",
            slow=True)
        DMA("sync", bsl(b2b, 0, 1, 0, 16), bass.AP(b_f2.tensor, 0, [[1, 16], [1, 1]]),
            [dbuf(b_f2)], [b2b], "c")
        DMA("gpsimd", bap(w2b, 0, [[16, 16], [1, 16]]),
            bass.AP(w_f2.tensor, 0, [[16, 128], [16 * 128, 16], [1, 16]]),
            [dbuf(w_f2)], [w2b], "c")
        DMA("gpsimd", bap(wbb, 0, [[4, 8], [1, 4]]),
            bass.AP(w_b.tensor, 0, [[4, 128], [4 * 128, 8], [1, 4]]),
            [dbuf(w_b)], [wbb], "c")

        ltb = FA.alloc("lt", 8)
        DMA("sync", bsl(ltb, 0, 4, 0, 1), bass.AP(log_tau.tensor, 0, [[4, 1], [1, 4]]),
            [dbuf(log_tau)], [ltb], "c")
        OP("scalar", [ltb], [ltb],
           lambda e: e.activation(bsl(ltb, 4, 8, 0, 1), bsl(ltb, 0, 4, 0, 1), AF.Exp,
                                  bias=0.0, scale=-1.0), "rtau")
        OP("tensor", [ltb, onesf], [psb["A"]],
           lambda e: e.matmul(ps_t["A"][:, 0:4], bsl(onesf, 0, 128, 0, 1),
                              bsl(ltb, 4, 8, 0, 1), start=True, stop=True), "rtau")
        OP("vector", [], [rtau],
           lambda e: e.memset(bap(rtau, 3, [[4, 4]]), 1.0), "rtau")
        for j in range(3):
            OP("vector", [psb["A"]], [rtau],
               lambda e, j=j: e.tensor_copy(bap(rtau, j, [[4, 4]]), ps_t["A"][:, 0:4]), "rtau")
        FA.free(ltb)

        nw1 = FA.alloc("nw1", 256)
        nw1b = BA.alloc("nw1b", 256)
        DMA("sync", bsl(nw1, 0, 256, 0, 1),
            bass.AP(norm_w_in.tensor, 0, [[256, 1], [1, 256]]),
            [dbuf(norm_w_in)], [nw1], "c")
        OP("vector", [nw1], [nw1b],
           lambda e: e.tensor_copy(bsl(nw1b, 0, 256, 0, 1), bsl(nw1, 0, 256, 0, 1)), "nw")
        OP("tensor", [nw1b, onesb], [psb["C"]],
           lambda e: e.matmul(ps_t["C"][:, 0:256], bsl(onesb, 0, 128, 0, 1),
                              bsl(nw1b, 0, 256, 0, 1), start=True, stop=True), "nw")
        OP("vector", [psb["C"]], [nwrep],
           lambda e: e.tensor_copy(bsl(nwrep, 0, 256), ps_t["C"][:, 0:256]), "nw")
        FA.free(nw1); BA.free(nw1b)

        OP("vector", [], [S_f], lambda e: e.memset(bsl(S_f, 0, 8 * 512), 0.0), "init")
        OP("vector", [], [S_b], lambda e: e.memset(bsl(S_b, 0, 8 * 512), 0.0), "init")
        OP("vector", [], [qtails], lambda e: e.memset(bsl(qtails, 0, 3 * 8 * 2 * 4), 0.0), "init")
        OP("vector", [], [vtails], lambda e: e.memset(bsl(vtails, 0, 8 * 2 * 30), 0.0), "init")

        qtb["acc"] = [FA.alloc(f"qt_acc{i}", 2) for i in range(3)]
        qtb["sc1"] = FA.alloc("qt_sc1", 4)
        qtb["cb"] = [FA.alloc("qt_cb0", 1024), FA.alloc("qt_cb1", 1024)]
        qtb["t5"] = FA.alloc("qt_t5", 1024)
        qtb["mb"] = FA.alloc("qt_mb", 1024)
        qtb["qb"] = [BA.alloc("qt_qb0", 1024), BA.alloc("qt_qb1", 1024)]
        qtb["qbT"] = [BA.alloc("qt_qbT0", 1024), BA.alloc("qt_qbT1", 1024)]
        stage = BA.alloc("stage", 8192)
        def cast_big(key, rows, cols):
            src, scr = w_hbm[key], s_w[key]
            rows_per = min(max(1, 8192 // cols) * 128, rows)
            nk = rows_per // 128
            for r0 in range(0, rows, rows_per):
                DMA("gpsimd", bap(stage, 0, [[cols, nk], [1, cols]]),
                    bass.AP(src.tensor, r0 * cols, [[cols, 128], [128 * cols, nk], [1, cols]]),
                    [dbuf(src)], [stage], "cast")
                DMA("sync",
                    bass.AP(scr.tensor, r0 * cols, [[cols, 128], [128 * cols, nk], [1, cols]]),
                    bap(stage, 0, [[cols, nk], [1, cols]]),
                    [stage], [dbuf(scr)], "cast")
        for key in ("q", "k", "v", "o"):
            cast_big(key, D, D)
        cast_big("f1", INNER, DHID)
        BA.free(stage)

        sc1 = qtb["sc1"]
        t5, mb = qtb["t5"], qtb["mb"]
        wq_cfg = ((0, "g", D, INNER), (1, "u", D, INNER), (2, "d", INNER, D))

        def u_p1(scol, key, KIN, NOUT, kc2):
            src, acc = ffn_hbm[key], qtb["acc"][scol]
            WIDTH = KIN
            SUBW = min(WIDTH, 1024)
            NSUB = WIDTH // SUBW
            kc, sub = kc2 // NSUB, kc2 % NSUB
            cb = qtb["cb"][kc2 % 2]
            if kc2 == 0:
                OP("vector", [], [acc], lambda e, a=acc: e.memset(bsl(a, 0, 1), 0.0), "wq")
            DMA("sync", bsl(cb, 0, SUBW),
                bass.AP(src.tensor, kc * 128 * WIDTH + sub * SUBW,
                        [[WIDTH, 128], [1, SUBW]]),
                [dbuf(src)], [cb], "wq")
            OP("scalar", [cb], [t5, acc],
               lambda e, cb=cb, a=acc, S=SUBW: e.activation(
                   bsl(t5, 0, S), bsl(cb, 0, S), AF.Abs, bias=0.0, scale=1.0,
                   accum_out=bsl(a, 1, 2)), "wq")
            OP("vector", [acc], [acc],
               lambda e, a=acc: e.tensor_tensor(bsl(a, 0, 1), bsl(a, 0, 1),
                                                bsl(a, 1, 2), ALU.add), "wq")

        def u_fin(scol, key, KIN, NOUT):
            acc = qtb["acc"][scol]
            OP("tensor", [acc, onesf], [psb["A"]],
               lambda e, a=acc: e.matmul(ps_t["A"][0:1, 0:1], bsl(onesf, 128, 129),
                                         bsl(a, 0, 1), start=True, stop=True), "wq")
            OP("vector", [psb["A"]], [sc1],
               lambda e, nn=1.0 / (KIN * NOUT): e.tensor_scalar(
                   bsl(sc1, 0, 1, 0, 1), ps_t["A"][0:1, 0:1], nn, 1e-5,
                   ALU.mult, ALU.max), "wq")
            OP("vector", [sc1], [sc1],
               lambda e: e.reciprocal(bsl(sc1, 1, 2, 0, 1), bsl(sc1, 0, 1, 0, 1)), "wq")
            OP("vector", [sc1], [sc1],
               lambda e: e.tensor_scalar(bsl(sc1, 2, 3, 0, 1), bsl(sc1, 0, 1, 0, 1),
                                         1.0 / 127.0, None, ALU.mult), "wq")
            OP("tensor", [sc1, onesf], [psb["B"]],
               lambda e: e.matmul(ps_t["B"][:, 0:2], bsl(onesf, 0, 128, 0, 1),
                                  bsl(sc1, 1, 3, 0, 1), start=True, stop=True), "wq")
            OP("vector", [psb["B"]], [ffsc],
               lambda e, sc=scol: e.tensor_copy(bsl(ffsc, sc * 2, sc * 2 + 2),
                                                ps_t["B"][:, 0:2]), "wq")

        def u_p2(scol, key, KIN, NOUT, kc2):
            src, scr = ffn_hbm[key], s_w[key]
            WIDTH = KIN
            SUBW = min(WIDTH, 1024)
            NSUB = WIDTH // SUBW
            nic = SUBW // 128
            kc, sub = kc2 // NSUB, kc2 % NSUB
            cb = qtb["cb"][kc2 % 2]
            qb = qtb["qb"][kc2 % 2]
            qbT = qtb["qbT"][kc2 % 2]
            DMA("sync", bsl(cb, 0, SUBW),
                bass.AP(src.tensor, kc * 128 * WIDTH + sub * SUBW,
                        [[WIDTH, 128], [1, SUBW]]),
                [dbuf(src)], [cb], "wq")
            OP("scalar", [cb, ffsc], [t5],
               lambda e, cb=cb, sc=scol: e.activation(
                   bsl(t5, 0, 1024), bsl(cb, 0, 1024), AF.Abs,
                   bias=0.0, scale=bsl(ffsc, sc * 2, sc * 2 + 1)), "wq")
            OP("scalar", [cb], [mb],
               lambda e, cb=cb: e.activation(bsl(mb, 0, 1024), bsl(cb, 0, 1024),
                                             AF.Sign, bias=0.0, scale=1.0), "wq")
            OP("vector", [t5], [t5],
               lambda e: e.tensor_scalar(bsl(t5, 0, 1024), bsl(t5, 0, 1024),
                                         0.5, None, ALU.is_ge), "wq")
            OP("vector", [t5, mb], [qb],
               lambda e, qb=qb: e.tensor_tensor(bsl(qb, 0, 1024), bsl(t5, 0, 1024),
                                                bsl(mb, 0, 1024), ALU.mult), "wq")
            DMA("sync", bap(qbT, 0, [[128, nic], [1, 128]]),
                bsl(qb, 0, SUBW), [qb], [qbT], "wq", transpose=True)
            DMA("sync",
                bass.AP(scr.tensor, (sub * SUBW) * NOUT + kc * 128,
                        [[NOUT, 128], [128 * NOUT, nic], [1, 128]]),
                bap(qbT, 0, [[128, nic], [1, 128]]),
                [qbT], [dbuf(scr)], "wq")

        for scol, key, KIN, NOUT in wq_cfg:
            n_units = (NOUT // 128) * (KIN // min(KIN, 1024))
            for kc2 in range(n_units):
                bgq.append(lambda s=scol, k=key, a=KIN, b=NOUT, c=kc2: u_p1(s, k, a, b, c))
        for scol, key, KIN, NOUT in wq_cfg:
            bgq.append(lambda s=scol, k=key, a=KIN, b=NOUT: u_fin(s, k, a, b))
        for scol, key, KIN, NOUT in wq_cfg:
            n_units = (NOUT // 128) * (KIN // min(KIN, 1024))
            for kc2 in range(n_units):
                bgq.append(lambda s=scol, k=key, a=KIN, b=NOUT, c=kc2: u_p2(s, k, a, b, c))

    def ln_chunk(x_ap, x_ap_lo, x_ap_hi, out_ap, stats, scol, out_buf):
        OP("vector", [x_tok], [stats],
           lambda e, a=x_ap_lo, s6=bsl(stats, scol, scol + 6): e.bn_stats(s6, a), "ln")
        OP("vector", [x_tok], [stats],
           lambda e, a=x_ap_hi, s6=bsl(stats, scol + 6, scol + 12): e.bn_stats(s6, a), "ln")
        OP("vector", [stats], [stats],
           lambda e, s12=bsl(stats, scol, scol + 12),
                  mv=bsl(stats, scol + 12, scol + 14): e.bn_aggr(mv, s12), "ln")
        OP("scalar", [stats, epsb], [stats],
           lambda e, i=bsl(stats, scol + 13, scol + 14),
                  o=bsl(stats, scol + 14, scol + 15): e.activation(
                      o, i, AF.Sqrt, bias=bsl(epsb, 0, 1), scale=1.0), "ln")
        OP("vector", [stats], [stats],
           lambda e, i=bsl(stats, scol + 14, scol + 15),
                  o=bsl(stats, scol + 15, scol + 16): e.reciprocal(o, i), "ln")
        OP("vector", [stats], [stats],
           lambda e, m=bsl(stats, scol + 12, scol + 13),
                  r=bsl(stats, scol + 15, scol + 16),
                  o=bsl(stats, scol + 16, scol + 17): e.scalar_tensor_tensor(
                      o, m, -1.0, r, ALU.mult, ALU.mult), "ln")
        OP("scalar", [x_tok, stats], [out_buf],
           lambda e, a=x_ap, o=out_ap,
                  bi=bsl(stats, scol + 16, scol + 17),
                  sc=bsl(stats, scol + 15, scol + 16): e.activation(
                      o, a, AF.Identity, bias=bi, scale=sc), "ln")

    def emit_h(h_feat, stats, apply_only=False):
        htokb = [BA.alloc("htok0", 1024), BA.alloc("htok1", 1024)]
        for tc in range(TCH):
            hb = htokb[tc % 2]
            if apply_only:
                scol = tc * 24
                OP("scalar", [x_tok, stats], [hb],
                   lambda e, a=bsl(x_tok, tc * 1024, (tc + 1) * 1024),
                          o=bsl(hb, 0, 1024),
                          bi=bsl(stats, scol + 16, scol + 17),
                          sc=bsl(stats, scol + 15, scol + 16): e.activation(
                              o, a, AF.Identity, bias=bi, scale=sc), "ln")
            else:
                ln_chunk(bsl(x_tok, tc * 1024, (tc + 1) * 1024),
                         bsl(x_tok, tc * 1024, tc * 1024 + 512),
                         bsl(x_tok, tc * 1024 + 512, (tc + 1) * 1024),
                         bsl(hb, 0, 1024), stats, tc * 24, hb)
            DMA("sync", bap(h_feat, tc * 128, [[TT, 8], [1, 128]]),
                bsl(hb, 0, 1024), [hb], [h_feat], "hT", transpose=True)
        BA.free(htokb[0]); BA.free(htokb[1])

    def make_prep_segments(si, hi, bufs, segs):
        (q_feat, k_feat, k_tok, v_tok, do_tok, beta_t, dwork,
         Gsb, t1w, t2w, Tbd, Ttf, Tbdt, locT, Aw, Bw, Pw, Ptw,
         u_b, what, kbw, vbw, uadj, rsp, rowT, RSJ, BRSJ, BRSJb) = bufs

        def qk_ap(buf, half, n):
            base = (2 * hi + half) * TT + si * TPER + n * 128
            return bsl(buf, base, base + 128)

        def tok_sl(buf, n, w=256):
            base = (si * TCS + n) * 1024 + hi * 256
            return bsl(buf, base, base + w)

        rs_k = lambda n: bsl(dwork, WD + n, WD + n + 1)
        brs = lambda n: bsl(dwork, WD + 2 * NCH + n, WD + 2 * NCH + n + 1)
        bcol = lambda n: bsl(beta_t, (si * TCS + n) * 4 + hi, (si * TCS + n) * 4 + hi + 1)

        def st_gram():
            for which, fbuf, ro in (("k", k_feat, 0), ("q", q_feat, NCH)):
                def em_g(e, fbuf=fbuf):
                    r = None
                    for n in range(NCH):
                        for half in range(2):
                            r = e.matmul(ps_t["A"][:, n * 128:(n + 1) * 128],
                                         qk_ap(fbuf, half, n), qk_ap(fbuf, half, n),
                                         start=(half == 0), stop=(half == 1))
                    return r
                OP("tensor", [fbuf], [psb["A"]], em_g, "gram")
                if which == "k":
                    OP("vector", [psb["A"]], [Gsb],
                       lambda e: e.tensor_copy(bsl(Gsb, 0, WD), ps_t["A"][:, 0:WD]), "gram")
                OP("vector", [psb["A"], cstb["ident"]], [dwork],
                   lambda e: e.tensor_tensor(bsl(dwork, 0, WD), ps_t["A"][:, 0:WD],
                                             cap_rep("ident", NCH), ALU.mult), "rs")
                OP("vector", [dwork], [dwork],
                   lambda e, ro=ro: e.reduce_sum(
                       bap(dwork, WD + 3 * NCH + ro, [[1, NCH], [1, 1]]),
                       bap(dwork, 0, [[128, NCH], [1, 128]]), AXX), "rs")
                OP("scalar", [dwork, epsb], [dwork],
                   lambda e, ro=ro: e.activation(
                       bsl(dwork, WD + 5 * NCH + ro, WD + 5 * NCH + ro + NCH),
                       bsl(dwork, WD + 3 * NCH + ro, WD + 3 * NCH + ro + NCH),
                       AF.Sqrt, bias=bsl(epsb, 1, 2), scale=1.0), "rs")
                OP("vector", [dwork], [dwork],
                   lambda e, ro=ro: e.reciprocal(
                       bsl(dwork, WD + ro, WD + ro + NCH),
                       bsl(dwork, WD + 5 * NCH + ro, WD + 5 * NCH + ro + NCH)), "rs")
            OP("vector", [beta_t, dwork], [dwork],
               lambda e: e.tensor_tensor(
                   bsl(dwork, WD + 2 * NCH, WD + 3 * NCH),
                   bap(beta_t, (si * TCS) * 4 + hi, [[4, NCH]]),
                   bsl(dwork, WD, WD + NCH), ALU.mult), "rs")
        segs.append(st_gram)

        def st_ktn():
            for n in range(NCH):
                OP("vector", [k_tok, dwork], [k_tok],
                   lambda e, n=n: e.tensor_scalar(tok_sl(k_tok, n), tok_sl(k_tok, n),
                                                  rs_k(n), None, ALU.mult), "ktn")
                OP("vector", [k_tok, beta_t, dwork], [kbw],
                   lambda e, n=n: e.tensor_scalar(bsl(kbw, n * 256, (n + 1) * 256),
                                                  tok_sl(k_tok, n), bcol(n), None, ALU.mult), "ktn")
                OP("vector", [v_tok, beta_t], [vbw],
                   lambda e, n=n: e.tensor_scalar(bsl(vbw, n * 256, (n + 1) * 256),
                                                  tok_sl(v_tok, n), bcol(n), None, ALU.mult), "ktn")
        segs.append(st_ktn)

        def st_bc():
            OP("vector", [dwork], [rsp],
               lambda e: e.tensor_copy(bsl(rsp, 0, NCH), bsl(dwork, WD, WD + NCH)), "bc")
            OP("vector", [dwork], [rsp],
               lambda e: e.tensor_copy(bsl(rsp, NCH, 2 * NCH),
                                       bsl(dwork, WD + 2 * NCH, WD + 3 * NCH)), "bc")
            OP("tensor", [rsp, cstb["ident"]], [psb["T"]],
               lambda e: e.transpose(ps_t["T"][0:2 * NCH, 0:128],
                                     bsl(rsp, 0, 2 * NCH), cap("ident")), "bc")
            OP("vector", [psb["T"]], [rowT],
               lambda e: e.tensor_copy(bsl(rowT, 0, 128, 0, 2 * NCH),
                                       ps_t["T"][0:2 * NCH, 0:128]), "bc")
            def sel_ap(n):
                return bap(cstb["ident"], n, [[0, 128]], 0, 2 * NCH)
            def em_bc(e):
                r = None
                for n in range(NCH):
                    r = e.matmul(ps_t["D"][:, n * 128:(n + 1) * 128], sel_ap(n),
                                 bsl(rowT, 0, 128, 0, 2 * NCH), start=True, stop=True)
                return r
            OP("tensor", [rowT, cstb["ident"]], [psb["D"]], em_bc, "bc")
            OP("vector", [psb["D"], cstb["mT_bd"]], [RSJ],
               lambda e: e.tensor_tensor(bsl(RSJ, 0, WD), ps_t["D"][:, 0:WD],
                                         cap_rep("mT_bd", NCH), ALU.mult), "bc")
            def em_bc2(e):
                r = None
                for n in range(NCH):
                    r = e.matmul(ps_t["D"][:, n * 128:(n + 1) * 128], sel_ap(NCH + n),
                                 bsl(rowT, 0, 128, 0, 2 * NCH), start=True, stop=True)
                return r
            OP("tensor", [rowT, cstb["ident"]], [psb["D"]], em_bc2, "bc")
            OP("vector", [psb["D"], cstb["mTt"]], [BRSJ],
               lambda e: e.tensor_tensor(bsl(BRSJ, 0, WD), ps_t["D"][:, 0:WD],
                                         cap_rep("mTt", NCH), ALU.mult), "bc")
        segs.append(st_bc)

        def st_T():
            for n in range(NCH):
                OP("vector", [Gsb, dwork], [t1w],
                   lambda e, n=n: e.tensor_scalar(bsl(t1w, n * 128, (n + 1) * 128),
                                                  bsl(Gsb, n * 128, (n + 1) * 128),
                                                  brs(n), None, ALU.mult), "T")
                OP("vector", [Gsb, dwork], [t2w],
                   lambda e, n=n: e.tensor_scalar(bsl(t2w, n * 128, (n + 1) * 128),
                                                  bsl(Gsb, n * 128, (n + 1) * 128),
                                                  rs_k(n), None, ALU.mult), "T")
            OP("vector", [t1w, RSJ], [Tbd],
               lambda e: e.tensor_tensor(bsl(Tbd, 0, WD), bsl(t1w, 0, WD),
                                         bsl(RSJ, 0, WD), ALU.mult), "T")
            OP("vector", [t2w, BRSJ], [Ttf],
               lambda e: e.tensor_tensor(bsl(Ttf, 0, WD), bsl(t2w, 0, WD),
                                         bsl(BRSJ, 0, WD), ALU.mult), "T")
            def em_tbdt(e):
                r = None
                for n in range(NCH):
                    r = e.transpose(ps_t["T"][:, n * 128:(n + 1) * 128],
                                    bsl(Tbd, n * 128, (n + 1) * 128), cap("ident"))
                return r
            OP("tensor", [Tbd, cstb["ident"]], [psb["T"]], em_tbdt, "T")
            OP("vector", [psb["T"]], [Tbdt],
               lambda e: e.tensor_copy(bsl(Tbdt, 0, WD), ps_t["T"][:, 0:WD]), "T")
        segs.append(st_T)

        def st_loc():
            def em_loc(e):
                r = None
                for n in range(NCH):
                    for half in range(2):
                        r = e.matmul(ps_t["A"][:, n * 128:(n + 1) * 128],
                                     qk_ap(k_feat, half, n), qk_ap(q_feat, half, n),
                                     start=(half == 0), stop=(half == 1))
                return r
            OP("tensor", [k_feat, q_feat], [psb["A"]], em_loc, "loc")
            for n in range(NCH):
                OP("vector", [psb["A"], dwork], [t1w],
                   lambda e, n=n: e.tensor_scalar(bsl(t1w, n * 128, (n + 1) * 128),
                                                  ps_t["A"][:, n * 128:(n + 1) * 128],
                                                  rs_k(n), None, ALU.mult), "loc")
            OP("vector", [t1w, cstb["mloc"]], [locT],
               lambda e: e.tensor_tensor(bsl(locT, 0, WD), bsl(t1w, 0, WD),
                                         cap_rep("mloc", NCH), ALU.mult), "loc")
        segs.append(st_loc)

        def st_ainit():
            OP("vector", [Tbd, cstb["ident"]], [Aw],
               lambda e: e.tensor_tensor(bsl(Aw, 0, WD), bsl(Tbd, 0, WD),
                                         cap_rep("ident", NCH), ALU.add), "inv")
        segs.append(st_ainit)

        chain = [(Tbd, Tbdt), (Pw, Ptw)]
        for lev in range(4):
            Pcur, Ptcur = chain[lev % 2]
            Pnxt, Ptnxt = chain[(lev + 1) % 2]
            last = (lev == 3)
            def st_lev(Pcur=Pcur, Ptcur=Ptcur, Pnxt=Pnxt, Ptnxt=Ptnxt, last=last):
                if not last:
                    def em_p2(e):
                        r = None
                        for n in range(NCH):
                            r = e.matmul(ps_t["A"][:, n * 128:(n + 1) * 128],
                                         bsl(Ptcur, n * 128, (n + 1) * 128),
                                         bsl(Pcur, n * 128, (n + 1) * 128),
                                         start=True, stop=True)
                        return r
                    OP("tensor", [Pcur, Ptcur], [psb["A"]], em_p2, "inv")
                def em_p2t(e):
                    r = None
                    for n in range(NCH):
                        r = e.matmul(ps_t["B"][:, n * 128:(n + 1) * 128],
                                     bsl(Pcur, n * 128, (n + 1) * 128),
                                     bsl(Ptcur, n * 128, (n + 1) * 128),
                                     start=True, stop=True)
                    return r
                OP("tensor", [Pcur, Ptcur], [psb["B"]], em_p2t, "inv")
                if not last:
                    OP("vector", [psb["A"]], [Pnxt],
                       lambda e: e.tensor_copy(bsl(Pnxt, 0, WD), ps_t["A"][:, 0:WD]), "inv")
                OP("vector", [psb["B"]], [Ptnxt],
                   lambda e: e.tensor_copy(bsl(Ptnxt, 0, WD), ps_t["B"][:, 0:WD]), "inv")
                def em_aup(e):
                    r = None
                    for n in range(NCH):
                        r = e.matmul(ps_t["E"][:, n * 128:(n + 1) * 128],
                                     bsl(Ptnxt, n * 128, (n + 1) * 128),
                                     bsl(Aw, n * 128, (n + 1) * 128), start=True, stop=True)
                    return r
                OP("tensor", [Ptnxt, Aw], [psb["E"]], em_aup, "inv")
                OP("vector", [psb["E"], Aw], [Aw],
                   lambda e: e.tensor_tensor(bsl(Aw, 0, WD), bsl(Aw, 0, WD),
                                             ps_t["E"][:, 0:WD], ALU.add), "inv")
            segs.append(st_lev)

        def st_b32():
            def em_b32(e):
                r = None
                for n in range(NCH):
                    r = e.transpose(ps_t["T"][:, n * 128:(n + 1) * 128],
                                    bsl(Aw, n * 128, (n + 1) * 128), cap("ident"))
                return r
            OP("tensor", [Aw, cstb["ident"]], [psb["T"]], em_b32, "inv")
            OP("vector", [psb["T"]], [Bw],
               lambda e: e.tensor_copy(bsl(Bw, 0, WD), ps_t["T"][:, 0:WD]), "inv")
        segs.append(st_b32)

        Ctw, Yw, Xw = Gsb, t1w, t2w
        for mname, paste in (("mC32", [(32, 64, 0), (96, 128, 64)]),
                             ("mC64", [(64, 128, 0)])):
            def st_mrg(mname=mname, paste=paste):
                OP("vector", [Ttf, cstb[mname]], [Ctw],
                   lambda e: e.tensor_tensor(bsl(Ctw, 0, WD), bsl(Ttf, 0, WD),
                                             cap_rep(mname, NCH), ALU.mult), "mrg")
                def em_y(e):
                    r = None
                    for n in range(NCH):
                        r = e.matmul(ps_t["A"][:, n * 128:(n + 1) * 128],
                                     bsl(Ctw, n * 128, (n + 1) * 128),
                                     bsl(Aw, n * 128, (n + 1) * 128), start=True, stop=True)
                    return r
                OP("tensor", [Ctw, Aw], [psb["A"]], em_y, "mrg")
                OP("vector", [psb["A"]], [Yw],
                   lambda e: e.tensor_copy(bsl(Yw, 0, WD), ps_t["A"][:, 0:WD]), "mrg")
                def em_x(e):
                    r = None
                    for n in range(NCH):
                        r = e.matmul(ps_t["B"][:, n * 128:(n + 1) * 128],
                                     bsl(Bw, n * 128, (n + 1) * 128),
                                     bsl(Yw, n * 128, (n + 1) * 128), start=True, stop=True)
                    return r
                OP("tensor", [Bw, Yw], [psb["B"]], em_x, "mrg")
                OP("vector", [psb["B"]], [Xw],
                   lambda e: e.tensor_copy(bsl(Xw, 0, WD), ps_t["B"][:, 0:WD]), "mrg")
                for (r0, r1, c0) in paste:
                    wdt = r1 - r0
                    OP("vector", [psb["B"], Aw], [Aw],
                       lambda e, r0=r0, r1=r1, c0=c0, wdt=wdt: e.tensor_tensor(
                           bap(Aw, c0, [[128, NCH], [1, wdt]], p0=r0, pn=r1 - r0),
                           bap(Aw, c0, [[128, NCH], [1, wdt]], p0=r0, pn=r1 - r0),
                           bass.AP(ps_t["B"], r0 * 512 + c0, [[512, wdt], [128, NCH], [1, wdt]]),
                           ALU.add), "mrg")
                def em_xt(e):
                    r = None
                    for n in range(NCH):
                        r = e.transpose(ps_t["T"][:, n * 128:(n + 1) * 128],
                                        bsl(Xw, n * 128, (n + 1) * 128), cap("ident"))
                    return r
                OP("tensor", [Xw, cstb["ident"]], [psb["T"]], em_xt, "mrg")
                for (r0, r1, c0) in paste:
                    wdt = r1 - r0
                    OP("vector", [psb["T"], Bw], [Bw],
                       lambda e, r0=r0, r1=r1, c0=c0, wdt=wdt: e.tensor_tensor(
                           bap(Bw, r0, [[128, NCH], [1, wdt]], p0=c0, pn=wdt),
                           bap(Bw, r0, [[128, NCH], [1, wdt]], p0=c0, pn=wdt),
                           bass.AP(ps_t["T"], c0 * 1024 + r0, [[1024, wdt], [128, NCH], [1, wdt]]),
                           ALU.add), "mrg")
            segs.append(st_mrg)

        def st_uw():
            def em_u(e):
                r = None
                for n in range(NCH):
                    bank = "A" if n < 2 else "B"
                    r = e.matmul(ps_t[bank][:, (n % 2) * 256:(n % 2) * 256 + 256],
                                 bsl(Bw, n * 128, (n + 1) * 128),
                                 bsl(vbw, n * 256, (n + 1) * 256), start=True, stop=True)
                return r
            OP("tensor", [Bw, vbw], [psb["A"], psb["B"]], em_u, "uw")
            OP("vector", [psb["A"]], [u_b],
               lambda e, w=min(512, NCH * 256): e.tensor_copy(bsl(u_b, 0, w),
                                                              ps_t["A"][:, 0:w]), "uw")
            if NCH > 2:
                OP("vector", [psb["B"]], [u_b],
                   lambda e: e.tensor_copy(bsl(u_b, 512, 1024), ps_t["B"][:, 0:512]), "uw")
            def em_w(e):
                r = None
                for n in range(NCH):
                    bank = "E" if n < 2 else "B"
                    for half in range(2):
                        cc = (n % 2) * 256 + half * 128
                        r = e.matmul(ps_t[bank][:, cc:cc + 128],
                                     bsl(kbw, n * 256 + half * 128, n * 256 + half * 128 + 128),
                                     bsl(Bw, n * 128, (n + 1) * 128), start=True, stop=True)
                return r
            OP("tensor", [kbw, Bw], [psb["E"], psb["B"]], em_w, "uw")
            OP("vector", [psb["E"]], [what],
               lambda e, w=min(512, NCH * 256): e.tensor_copy(bsl(what, 0, w),
                                                              ps_t["E"][:, 0:w]), "uw")
            if NCH > 2:
                OP("vector", [psb["B"]], [what],
                   lambda e: e.tensor_copy(bsl(what, 512, 1024), ps_t["B"][:, 0:512]), "uw")
        segs.append(st_uw)

    def emit_scan_step(si, hi, bufs, n):
        (q_feat, k_feat, k_tok, v_tok, do_tok, beta_t, dwork,
         Gsb, t1w, t2w, Tbd, Ttf, Tbdt, locT, Aw, Bw, Pw, Ptw,
         u_b, what, kbw, vbw, uadj, rsp, rowT, RSJ, BRSJ, BRSJb) = bufs
        pair = si * H + hi

        def qk_ap(buf, half, n):
            base = (2 * hi + half) * TT + si * TPER + n * 128
            return bsl(buf, base, base + 128)

        def tok_sl(buf, n, w=256):
            base = (si * TCS + n) * 1024 + hi * 256
            return bsl(buf, base, base + w)

        rs_q = lambda n: bsl(dwork, WD + NCH + n, WD + NCH + n + 1)
        Sfp = bsl(S_f, pair * 512, (pair + 1) * 512)

        def em_ws(e, n=n):
            r = None
            for half in range(2):
                r = e.matmul(ps_t["C"][:, 0:256],
                             bsl(what, n * 256 + half * 128, n * 256 + half * 128 + 128),
                             bsl(S_b, pair * 512 + half * 256, pair * 512 + half * 256 + 256),
                             start=(half == 0), stop=(half == 1))
            return r
        OP("tensor", [what, S_b], [psb["C"]], em_ws, "scan")
        OP("vector", [u_b, psb["C"]], [uadj],
           lambda e, n=n: e.tensor_tensor(bsl(uadj, 0, 256),
                                          bsl(u_b, n * 256, (n + 1) * 256),
                                          ps_t["C"][:, 0:256], ALU.subtract), "scan")
        def em_o(e, n=n):
            for half in range(2):
                e.matmul(ps_t["G"][:, 0:256], qk_ap(q_feat, half, n),
                         bsl(S_b, pair * 512 + half * 256, pair * 512 + half * 256 + 256),
                         start=(half == 0), stop=False)
            return e.matmul(ps_t["G"][:, 0:256], bsl(locT, n * 128, (n + 1) * 128),
                            bsl(uadj, 0, 256), start=False, stop=True)
        OP("tensor", [q_feat, S_b, locT, uadj], [psb["G"]], em_o, "scan")
        def em_su(e, n=n):
            r = None
            for half in range(2):
                r = e.matmul(ps_t["F"][:, half * 256:half * 256 + 256],
                             bap(k_tok, (si * TCS + n) * 1024 + hi * 256 + half * 128,
                                 [[1, 128]]),
                             bsl(uadj, 0, 256), start=True, stop=True)
            return r
        OP("tensor", [k_tok, uadj], [psb["F"]], em_su, "scan")
        OP("vector", [psb["F"], S_f], [S_f],
           lambda e: e.tensor_tensor(Sfp, Sfp, ps_t["F"][:, 0:512], ALU.add), "scan")
        OP("vector", [S_f], [S_b],
           lambda e: e.tensor_copy(bsl(S_b, pair * 512, (pair + 1) * 512), Sfp), "scan")
        OP("vector", [psb["G"], dwork], [do_tok],
           lambda e, n=n: e.tensor_scalar(tok_sl(do_tok, n), ps_t["G"][:, 0:256],
                                          rs_q(n), None, ALU.mult), "scan")

    def firw_bc(col):
        return bap(firw, col, [[0, TPER]])

    def emit_fir(fcU, sU, v_pad, fs_feat, fl_feat, veng, ptmp=None):
        base = fcU * 2 * VP + sU * VP
        for (fbuf, taps, woff) in ((fl_feat, 31, 0), (fs_feat, 3, 31)):
            pad = taps - 1
            d0 = fcU * TT + sU * TPER
            if veng == "vector":
                OP(veng, [v_pad, firw], [fbuf],
                   lambda e, base=base, d0=d0, woff=woff, fcU=fcU, pad=pad, fbuf=fbuf:
                   e.tensor_scalar(bsl(fbuf, d0, d0 + TPER),
                                   bsl(v_pad, base + 30 - pad, base + 30 - pad + TPER),
                                   bsl(firw, fcU * 34 + woff, fcU * 34 + woff + 1),
                                   None, ALU.mult), "fir")
                for j in range(1, taps):
                    OP(veng, [v_pad, firw, fbuf], [fbuf],
                       lambda e, base=base, d0=d0, woff=woff, j=j, fcU=fcU, pad=pad, fbuf=fbuf:
                       e.scalar_tensor_tensor(
                           bsl(fbuf, d0, d0 + TPER),
                           bsl(v_pad, base + 30 - pad + j, base + 30 - pad + j + TPER),
                           bsl(firw, fcU * 34 + woff + j, fcU * 34 + woff + j + 1),
                           bsl(fbuf, d0, d0 + TPER), ALU.mult, ALU.add), "fir")
            else:
                OP(veng, [v_pad, firw], [fbuf],
                   lambda e, base=base, d0=d0, woff=woff, fcU=fcU, pad=pad, fbuf=fbuf:
                   e.tensor_tensor(bsl(fbuf, d0, d0 + TPER),
                                   bsl(v_pad, base + 30 - pad, base + 30 - pad + TPER),
                                   firw_bc(fcU * 34 + woff), ALU.mult), "fir")
                for j in range(1, taps):
                    OP(veng, [v_pad, firw], [ptmp],
                       lambda e, base=base, woff=woff, j=j, fcU=fcU, pad=pad, pt=ptmp:
                       e.tensor_tensor(bsl(pt, 0, TPER),
                                       bsl(v_pad, base + 30 - pad + j, base + 30 - pad + j + TPER),
                                       firw_bc(fcU * 34 + woff + j), ALU.mult), "fir")
                    OP(veng, [ptmp, fbuf], [fbuf],
                       lambda e, d0=d0, fbuf=fbuf, pt=ptmp:
                       e.tensor_tensor(bsl(fbuf, d0, d0 + TPER), bsl(fbuf, d0, d0 + TPER),
                                       bsl(pt, 0, TPER), ALU.add), "fir")

    def emit_tile(ti):
        t0 = ti * TPER
        for si in range(2):
            DMA("scalar", bap(x_tok, si * TCS * 1024, [[1024, TCS], [1, 1024]]),
                bass.AP(x_in.tensor, (si * L + t0) * D, [[D, 128], [128 * D, TCS], [1, D]]),
                [dbuf(x_in)], [x_tok], "xload")

        stats = FA.alloc("stats", TCH * 24)
        h_feat = BA.alloc("h_feat", 8 * TT)
        emit_h(h_feat, stats)
        pump(16)

        beta_t = FA.alloc("beta", TCH * 4)
        for tg in range(NTG):
            def em_beta(e, tg=tg):
                r = None
                for k in range(8):
                    r = e.matmul(ps_t["E"][0:4, 0:TGW],
                                 bap(wbb, k * 4, [[1, 4]]),
                                 bap(h_feat, k * TT + tg * TGW, [[1, TGW]]),
                                 start=(k == 0), stop=(k == 7))
                return r
            OP("tensor", [wbb, h_feat], [psb["E"]], em_beta, "beta")
            bfe = BA.alloc("bfe", TGW)
            OP("scalar", [psb["E"]], [bfe],
               lambda e, b=bfe: e.activation(bsl(b, 0, TGW, 0, 4), ps_t["E"][0:4, 0:TGW],
                                             AF.Identity, bias=0.0, scale=1.0), "beta")
            for c4 in range(TGW // 128):
                tc = tg * (TGW // 128) + c4
                OP("tensor", [bfe, cstb["ident"]], [psb["T"]],
                   lambda e, b=bfe, c4=c4, tc=tc: e.transpose(
                       ps_t["T"][:, tc * 4:(tc + 1) * 4],
                       bsl(b, c4 * 128, (c4 + 1) * 128, 0, 4),
                       cap("ident", 0, 4, 0, 4)), "beta")
            BA.free(bfe)
        OP("scalar", [psb["T"]], [beta_t],
           lambda e: e.activation(bsl(beta_t, 0, TCH * 4), ps_t["T"][:, 0:TCH * 4],
                                  AF.Sigmoid, bias=0.0, scale=1.0), "beta")
        pump(8)

        q_feat = BA.alloc("q_feat", 8 * TT)
        k_feat = BA.alloc("k_feat", 8 * TT)
        v_pad = BA.alloc("v_pad", 8 * 2 * VP)
        for fc in range(8):
            for si in range(2):
                OP("vector", [vtails], [v_pad],
                   lambda e, fc=fc, si=si: e.tensor_copy(
                       bsl(v_pad, fc * 2 * VP + si * VP, fc * 2 * VP + si * VP + 30),
                       bsl(vtails, (fc * 2 + si) * 30, (fc * 2 + si) * 30 + 30)), "vtl")
        wbufs = [BA.alloc("wbuf0", 8192)]
        qcb = BA.alloc("qc", 2 * QP)
        cdgb = [BA.alloc("cdg0", 4 * 128), BA.alloc("cdg1", 4 * 128)]
        for pi, dkey in enumerate(("q", "k", "v")):
            dst = {"q": q_feat, "k": k_feat, "v": v_pad}[dkey]
            wbuf = wbufs[0]
            DMA("sync", bap(wbuf, 0, [[1024, 8], [1, 1024]]),
                bass.AP(s_w[dkey].tensor, 0, [[D, 128], [128 * D, 8], [1, D]]),
                [dbuf(s_w[dkey])], [wbuf], "wqkv")
            for m in range(8):
                for tg in range(NTG):
                    bank = rotAD[(m * NTG + tg) % 4]
                    def em(e, m=m, tg=tg, bank=bank, wbuf=wbuf):
                        r = None
                        for k in range(8):
                            r = e.matmul(ps_t[bank][:, 0:TGW],
                                         bsl(wbuf, k * 1024 + m * 128, k * 1024 + (m + 1) * 128),
                                         bap(h_feat, k * TT + tg * TGW, [[1, TGW]]),
                                         start=(k == 0), stop=(k == 7))
                        return r
                    OP("tensor", [wbuf, h_feat], [psb[bank]], em, "proj")
                    for s2 in range(max(1, TGW // TPER)):
                        si = tg * max(1, TGW // TPER) + s2
                        OP("vector", [psb[bank]], [qcb],
                           lambda e, bank=bank, si=si, s2=s2: e.tensor_copy(
                               bsl(qcb, si * QP + 6, si * QP + 6 + TPER),
                               ps_t[bank][:, s2 * TPER:(s2 + 1) * TPER]), "proj")
                for si in range(2):
                    tix = ((pi * 8 + m) * 2 + si) * 4
                    OP("vector", [qtails], [qcb],
                       lambda e, si=si, tix=tix: e.tensor_copy(
                           bsl(qcb, si * QP + 3, si * QP + 6),
                           bsl(qtails, tix, tix + 3)), "conv")
                    OP("vector", [qcb], [qtails],
                       lambda e, si=si, tix=tix: e.tensor_copy(
                           bsl(qtails, tix, tix + 3),
                           bsl(qcb, si * QP + 3 + TPER, si * QP + 6 + TPER)), "conv")
                cdg = cdgb[m % 2]
                for j in range(4):
                    OP("vector", [cstb["ident"], convw], [cdg],
                       lambda e, m=m, pi=pi, j=j, cdg=cdg: e.tensor_scalar(
                           bsl(cdg, j * 128, (j + 1) * 128), cap("ident"),
                           bsl(convw, pi * 32 + m * 4 + j, pi * 32 + m * 4 + j + 1),
                           None, ALU.mult), "conv")
                for si in range(2):
                    cb2 = rotAD[(m * 2 + si) % 4]
                    def em_cv(e, si=si, cdg=cdg, cb2=cb2):
                        r = None
                        for j in range(4):
                            r = e.matmul(ps_t[cb2][:, 0:TPER],
                                         bsl(cdg, j * 128, (j + 1) * 128),
                                         bsl(qcb, si * QP + 3 + j, si * QP + 3 + j + TPER),
                                         start=(j == 0), stop=(j == 3))
                        return r
                    OP("tensor", [cdg, qcb], [psb[cb2]], em_cv, "conv")
                pump(8)
                if pi < 2:
                    for si in range(2):
                        cb2 = rotAD[(m * 2 + si) % 4]
                        OP("scalar", [psb[cb2]], [dst],
                           lambda e, d=dst, m=m, si=si, cb2=cb2: e.activation(
                               bsl(d, m * TT + si * TPER, m * TT + (si + 1) * TPER),
                               ps_t[cb2][:, 0:TPER],
                               AF.Silu, bias=0.0, scale=1.0), "silu")
                else:
                    for si in range(2):
                        cb2 = rotAD[(m * 2 + si) % 4]
                        OP("scalar", [psb[cb2]], [dst],
                           lambda e, d=dst, m=m, si=si, cb2=cb2: e.activation(
                               bsl(d, m * 2 * VP + si * VP + 30, m * 2 * VP + si * VP + 30 + TPER),
                               ps_t[cb2][:, 0:TPER],
                               AF.Silu, bias=0.0, scale=1.0), "silu")
        pump_flush()
        BA.free(wbufs[0]); BA.free(qcb); BA.free(cdgb[0]); BA.free(cdgb[1])
        BA.free(h_feat)

        k_tok = BA.alloc("k_tok", TCH * 1024)
        v_tok = BA.alloc("v_tok", TCH * 1024)
        for fc in range(8):
            for si in range(2):
                DMA("sync", bap(k_tok, si * TCS * 1024 + fc * 128, [[1024, TCS], [1, 128]]),
                    bsl(k_feat, fc * TT + si * TPER, fc * TT + (si + 1) * TPER),
                    [k_feat], [k_tok], "kT", transpose=True)
                DMA("sync", bap(v_tok, si * TCS * 1024 + fc * 128, [[1024, TCS], [1, 128]]),
                    bsl(v_pad, fc * 2 * VP + si * VP + 30, fc * 2 * VP + si * VP + 30 + TPER),
                    [v_pad], [v_tok], "vT", transpose=True)

        do_tok = BA.alloc("do_tok", TCH * 1024)
        dbufs = dict(
            Gsb=BA.alloc("Gsb", WD), t1w=BA.alloc("t1w", WD), t2w=BA.alloc("t2w", WD),
            Tbd=BA.alloc("Tbd", WD), Ttf=BA.alloc("Ttf", WD), Tbdt=BA.alloc("Tbdt", WD),
            Aw=BA.alloc("Aw", WD), Bw=BA.alloc("Bw", WD),
            kbw=BA.alloc("kbw", NCH * 256), vbw=BA.alloc("vbw", NCH * 256),
            uadj=BA.alloc("uadj", 256), rsp=BA.alloc("rsp", 2 * NCH),
            rowT=BA.alloc("rowT", 128), RSJ=BA.alloc("RSJ", WD),
            BRSJ=BA.alloc("BRSJ", WD))
        slots = [dict(dwork=FA.alloc(f"dwork{i}", WD + 7 * NCH),
                      locT=BA.alloc(f"locT{i}", WD),
                      u_b=BA.alloc(f"u_b{i}", NCH * 256),
                      what=BA.alloc(f"what{i}", NCH * 256)) for i in range(2)]

        def pair_bufs(p):
            sl = slots[p % 2]
            return (q_feat, k_feat, k_tok, v_tok, do_tok, beta_t, sl["dwork"],
                    dbufs["Gsb"], dbufs["t1w"], dbufs["t2w"], dbufs["Tbd"],
                    dbufs["Ttf"], dbufs["Tbdt"], sl["locT"], dbufs["Aw"],
                    dbufs["Bw"], dbufs["RSJ"], dbufs["BRSJ"], sl["u_b"],
                    sl["what"], dbufs["kbw"], dbufs["vbw"], dbufs["uadj"],
                    dbufs["rsp"], dbufs["rowT"], dbufs["RSJ"], dbufs["BRSJ"],
                    dbufs["BRSJ"])

        segs0 = []
        make_prep_segments(0, 0, pair_bufs(0), segs0)
        for fn in segs0:
            fn()
        for p in range(8):
            si, hi = divmod(p, H)
            nsegs = []
            if p + 1 < 8:
                make_prep_segments((p + 1) // H, (p + 1) % H, pair_bufs(p + 1), nsegs)
            per = -(-len(nsegs) // NCH) if nsegs else 0
            k = 0
            for n in range(NCH):
                emit_scan_step(si, hi, pair_bufs(p), n)
                for _ in range(per):
                    if k < len(nsegs):
                        nsegs[k]()
                        k += 1
            while k < len(nsegs):
                nsegs[k]()
                k += 1
        for b in dbufs.values():
            BA.free(b)
        for sl in slots:
            FA.free(sl["dwork"])
            BA.free(sl["locT"]); BA.free(sl["u_b"]); BA.free(sl["what"])
        BA.free(q_feat)
        BA.free(k_tok)
        BA.free(k_feat)

        fs_feat = BA.alloc("fs_feat", 8 * TT)
        fl_feat = BA.alloc("fl_feat", 8 * TT)
        dgb = [BA.alloc("dg0", 34 * 128), BA.alloc("dg1", 34 * 128)]
        for fc in range(6):
            dg = dgb[fc % 2]
            for j in range(34):
                OP("vector", [cstb["ident"], firw], [dg],
                   lambda e, fc=fc, j=j, dg=dg: e.tensor_scalar(
                       bsl(dg, j * 128, (j + 1) * 128), cap("ident"),
                       bsl(firw, fc * 34 + j, fc * 34 + j + 1), None, ALU.mult), "fird")
            for si in range(2):
                base = fc * 2 * VP + si * VP
                bl = rotAD[(fc * 2 + si) % 2 * 2]
                bs = rotAD[(fc * 2 + si) % 2 * 2 + 1]
                def em_firl(e, base=base, dg=dg, bl=bl):
                    r = None
                    for j in range(31):
                        r = e.matmul(ps_t[bl][:, 0:TPER],
                                     bsl(dg, j * 128, (j + 1) * 128),
                                     bsl(v_pad, base + j, base + j + TPER),
                                     start=(j == 0), stop=(j == 30))
                    return r
                OP("tensor", [dg, v_pad], [psb[bl]], em_firl, "fird")
                def em_firs(e, base=base, dg=dg, bs=bs):
                    r = None
                    for j in range(3):
                        r = e.matmul(ps_t[bs][:, 0:TPER],
                                     bsl(dg, (31 + j) * 128, (31 + j + 1) * 128),
                                     bsl(v_pad, base + 28 + j, base + 28 + j + TPER),
                                     start=(j == 0), stop=(j == 2))
                    return r
                OP("tensor", [dg, v_pad], [psb[bs]], em_firs, "fird")
                d0 = fc * TT + si * TPER
                OP("vector", [psb[bl]], [fl_feat],
                   lambda e, d0=d0, bl=bl: e.tensor_copy(
                       bsl(fl_feat, d0, d0 + TPER), ps_t[bl][:, 0:TPER]), "fird")
                OP("vector", [psb[bs]], [fs_feat],
                   lambda e, d0=d0, bs=bs: e.tensor_copy(
                       bsl(fs_feat, d0, d0 + TPER), ps_t[bs][:, 0:TPER]), "fird")
        BA.free(dgb[0]); BA.free(dgb[1])
        for uidx in range(12, 16):
            emit_fir(uidx // 2, uidx % 2, v_pad, fs_feat, fl_feat, "vector", None)
        for fc in range(8):
            for si in range(2):
                OP("vector", [v_pad], [vtails],
                   lambda e, fc=fc, si=si: e.tensor_copy(
                       bsl(vtails, (fc * 2 + si) * 30, (fc * 2 + si) * 30 + 30),
                       bsl(v_pad, fc * 2 * VP + si * VP + TPER, fc * 2 * VP + si * VP + TPER + 30)), "vtl")
        BA.free(v_pad)

        do_feat = BA.alloc("do_feat", 8 * TT)
        for tc in range(TCH):
            DMA("sync", bap(do_feat, tc * 128, [[TT, 8], [1, 128]]),
                bsl(do_tok, tc * 1024, (tc + 1) * 1024), [do_tok], [do_feat], "doT",
                transpose=True)

        h_feat2 = BA.alloc("h_feat2", 8 * TT)
        emit_h(h_feat2, stats, apply_only=True)

        w1ms = [BA.alloc("w1m0", 4096), BA.alloc("w1m1", 4096)]
        hidb = BA.alloc("hid", 2 * 1024)
        logf = BA.alloc("logf", TT)
        gmap = [(h_feat2, 0), (fs_feat, 8), (fl_feat, 16), (do_feat, 24)]
        def gate_ap(k, tg):
            for buf, k0 in reversed(gmap):
                if k >= k0:
                    return bap(buf, (k - k0) * TT + tg * TGW, [[1, TGW]])
        for m in range(16):
            w1m = w1ms[m % 2]
            DMA("sync", bap(w1m, 0, [[128, 32], [1, 128]]),
                bass.AP(s_w["f1"].tensor, m * 128, [[DHID, 128], [128 * DHID, 32], [1, 128]]),
                [dbuf(s_w["f1"])], [w1m], "w1")
            for tg in range(NTG):
                bank = rotAD[(m * NTG + tg) % 4]
                def em_f1(e, m=m, tg=tg, bank=bank, w1m=w1m):
                    r = None
                    for k in range(32):
                        r = e.matmul(ps_t[bank][:, 0:TGW],
                                     bsl(w1m, k * 128, (k + 1) * 128),
                                     gate_ap(k, tg), start=(k == 0), stop=(k == 31))
                    return r
                OP("tensor", [w1m, h_feat2, fs_feat, fl_feat, do_feat],
                   [psb[bank]], em_f1, "f1")
                OP("scalar", [psb[bank], b1b], [hidb],
                   lambda e, m=m, tg=tg, bank=bank: e.activation(
                       bsl(hidb, (m % 2) * 1024 + tg * TGW, (m % 2) * 1024 + tg * TGW + TGW),
                       ps_t[bank][:, 0:TGW], AF.Gelu,
                       bias=bsl(b1b, m, m + 1), scale=1.0), "f1")
            for tg in range(NTG):
                bank2 = "E" if tg == 0 else "G"
                OP("tensor", [hidb, w2b], [psb[bank2]],
                   lambda e, m=m, tg=tg, bank2=bank2: e.matmul(
                       ps_t[bank2][0:16, 0:TGW],
                       bap(w2b, m * 16, [[1, 16]]),
                       bsl(hidb, (m % 2) * 1024 + tg * TGW, (m % 2) * 1024 + tg * TGW + TGW),
                       start=(m == 0), stop=(m == 15)), "f2")
        for tg in range(NTG):
            bank2 = "E" if tg == 0 else "G"
            OP("scalar", [psb[bank2], b2b], [logf],
               lambda e, tg=tg, bank2=bank2: e.activation(
                   bsl(logf, tg * TGW, (tg + 1) * TGW, 0, 16),
                   ps_t[bank2][0:16, 0:TGW], AF.Identity,
                   bias=bsl(b2b, 0, 1, 0, 16), scale=1.0), "f2")
        BA.free(w1ms[0]); BA.free(w1ms[1]); BA.free(hidb); BA.free(h_feat2); BA.free(do_feat)

        fs_tok = BA.alloc("fs_tok", TCH * 1024)
        fl_tok = BA.alloc("fl_tok", TCH * 1024)
        for fc in range(8):
            for si in range(2):
                DMA("sync", bap(fs_tok, si * TCS * 1024 + fc * 128, [[1024, TCS], [1, 128]]),
                    bsl(fs_feat, fc * TT + si * TPER, fc * TT + (si + 1) * TPER),
                    [fs_feat], [fs_tok], "fsT", transpose=True)
                DMA("sync", bap(fl_tok, si * TCS * 1024 + fc * 128, [[1024, TCS], [1, 128]]),
                    bsl(fl_feat, fc * TT + si * TPER, fc * TT + (si + 1) * TPER),
                    [fl_feat], [fl_tok], "flT", transpose=True)
        BA.free(fs_feat); BA.free(fl_feat)

        wob = BA.alloc("wo", 8192)
        DMA("sync", bap(wob, 0, [[1024, 8], [1, 1024]]),
            bass.AP(s_w["o"].tensor, 0, [[D, 128], [128 * D, 8], [1, D]]),
            [dbuf(s_w["o"])], [wob], "wo")
        gwork = FA.alloc("gwork", 64)
        otok = BA.alloc("otok", 1024)
        orms = BA.alloc("orms", 2 * 1024)
        ofeatc = BA.alloc("ofeatc", 2 * 1024)
        sqw = FA.alloc("sq", 1024)
        for tc in range(TCH):
            OP("tensor", [logf, cstb["ident"]], [psb["T"]],
               lambda e, tc=tc: e.transpose(ps_t["T"][:, 0:16],
                                            bsl(logf, tc * 128, (tc + 1) * 128, 0, 16),
                                            cap("ident", 0, 16, 0, 16)), "gate")
            OP("vector", [psb["T"]], [gwork],
               lambda e: e.tensor_copy(bsl(gwork, 0, 16), ps_t["T"][:, 0:16]), "gate")
            OP("scalar", [gwork], [gwork],
               lambda e: e.activation(bsl(gwork, 16, 20), bap(gwork, 3, [[4, 4]]),
                                      AF.Sigmoid, bias=0.0, scale=1.0), "gate")
            OP("vector", [gwork], [gwork],
               lambda e: e.tensor_scalar(bsl(gwork, 20, 24), bsl(gwork, 16, 20),
                                         0.09, 0.9, ALU.mult, ALU.add), "gate")
            OP("vector", [gwork], [gwork],
               lambda e: e.tensor_tensor(bsl(gwork, 20, 24), bsl(gwork, 20, 24),
                                         bsl(gwork, 16, 20), ALU.mult), "gate")
            OP("vector", [gwork, rtau], [gwork],
               lambda e: e.tensor_tensor(bsl(gwork, 24, 40), bsl(gwork, 0, 16),
                                         bsl(rtau, 0, 16), ALU.mult), "gate")
            OP("vector", [gwork], [gwork],
               lambda e: e.tensor_tensor(bsl(gwork, 40, 44), bap(gwork, 24, [[4, 4]]),
                                         bap(gwork, 25, [[4, 4]]), ALU.max), "gate")
            OP("vector", [gwork], [gwork],
               lambda e: e.tensor_tensor(bsl(gwork, 40, 44), bsl(gwork, 40, 44),
                                         bap(gwork, 26, [[4, 4]]), ALU.max), "gate")
            OP("vector", [gwork], [gwork],
               lambda e: e.tensor_tensor(bap(gwork, 44, [[3, 4], [1, 3]]),
                                         bap(gwork, 24, [[4, 4], [1, 3]]),
                                         bap(gwork, 40, [[1, 4], [0, 3]]), ALU.subtract), "gate")
            OP("scalar", [gwork], [gwork],
               lambda e: e.activation(bsl(gwork, 44, 56), bsl(gwork, 44, 56),
                                      AF.Exp, bias=0.0, scale=1.0), "gate")
            OP("vector", [gwork], [gwork],
               lambda e: e.tensor_tensor(bsl(gwork, 56, 60), bap(gwork, 44, [[3, 4]]),
                                         bap(gwork, 45, [[3, 4]]), ALU.add), "gate")
            OP("vector", [gwork], [gwork],
               lambda e: e.tensor_tensor(bsl(gwork, 56, 60), bsl(gwork, 56, 60),
                                         bap(gwork, 46, [[3, 4]]), ALU.add), "gate")
            OP("vector", [gwork], [gwork],
               lambda e: e.reciprocal(bsl(gwork, 60, 64), bsl(gwork, 56, 60)), "gate")
            OP("vector", [gwork], [gwork],
               lambda e: e.tensor_scalar(bsl(gwork, 40, 44), bsl(gwork, 20, 24),
                                         -1.0, 1.0, ALU.mult, ALU.add), "gate")
            OP("vector", [gwork], [gwork],
               lambda e: e.tensor_tensor(bsl(gwork, 56, 60), bsl(gwork, 60, 64),
                                         bsl(gwork, 40, 44), ALU.mult), "gate")
            OP("vector", [gwork], [gwork],
               lambda e: e.tensor_tensor(bap(gwork, 44, [[3, 4], [1, 3]]),
                                         bap(gwork, 44, [[3, 4], [1, 3]]),
                                         bap(gwork, 56, [[1, 4], [0, 3]]), ALU.mult), "gate")
            for hh in range(H):
                OP("vector", [fs_tok, gwork], [otok],
                   lambda e, tc=tc, hh=hh: e.tensor_scalar(
                       bsl(otok, hh * 256, (hh + 1) * 256),
                       bsl(fs_tok, tc * 1024 + hh * 256, tc * 1024 + (hh + 1) * 256),
                       bsl(gwork, 44 + hh * 3, 45 + hh * 3), None, ALU.mult), "cmb")
                for (srcb, col) in ((fl_tok, 45 + hh * 3), (do_tok, 46 + hh * 3),
                                    (v_tok, 20 + hh)):
                    OP("vector", [srcb, gwork, otok], [otok],
                       lambda e, tc=tc, hh=hh, srcb=srcb, col=col: e.scalar_tensor_tensor(
                           bsl(otok, hh * 256, (hh + 1) * 256),
                           bsl(srcb, tc * 1024 + hh * 256, tc * 1024 + (hh + 1) * 256),
                           bsl(gwork, col, col + 1),
                           bsl(otok, hh * 256, (hh + 1) * 256), ALU.mult, ALU.add), "cmb")
            OP("vector", [otok], [sqw],
               lambda e: e.tensor_tensor(bsl(sqw, 0, 1024), bsl(otok, 0, 1024),
                                         bsl(otok, 0, 1024), ALU.mult), "rms")
            OP("vector", [sqw], [gwork],
               lambda e: e.reduce_sum(bap(gwork, 56, [[1, 4], [1, 1]]),
                                      bap(sqw, 0, [[256, 4], [1, 256]]), AXX), "rms")
            OP("scalar", [gwork], [gwork],
               lambda e: e.activation(bsl(gwork, 60, 64), bsl(gwork, 56, 60),
                                      AF.Sqrt, bias=bsl(epsb, 0, 1), scale=1.0 / 256.0), "rms")
            OP("vector", [gwork], [gwork],
               lambda e: e.reciprocal(bsl(gwork, 56, 60), bsl(gwork, 60, 64)), "rms")
            ob = (tc % 2) * 1024
            for hh in range(H):
                OP("vector", [otok, gwork, nwrep], [orms],
                   lambda e, hh=hh, ob=ob: e.scalar_tensor_tensor(
                       bsl(orms, ob + hh * 256, ob + (hh + 1) * 256),
                       bsl(otok, hh * 256, (hh + 1) * 256),
                       bsl(gwork, 56 + hh, 57 + hh), bsl(nwrep, 0, 256),
                       ALU.mult, ALU.mult), "rms")
            DMA("sync", bap(ofeatc, ob, [[128, 8], [1, 128]]),
                bsl(orms, ob, ob + 1024), [orms], [ofeatc], "oT", transpose=True)
            for fg in range(2):
                bank = rotAD[(tc * 2 + fg) % 4]
                def em_wo(e, tc=tc, fg=fg, bank=bank, ob=ob):
                    r = None
                    for k in range(8):
                        r = e.matmul(ps_t[bank][:, 0:512],
                                     bsl(ofeatc, ob + k * 128, ob + (k + 1) * 128),
                                     bsl(wob, k * 1024 + fg * 512, k * 1024 + (fg + 1) * 512),
                                     start=(k == 0), stop=(k == 7))
                    return r
                OP("tensor", [ofeatc, wob], [psb[bank]], em_wo, "wo")
                OP("vector", [psb[bank], x_tok], [x_tok],
                   lambda e, tc=tc, fg=fg, bank=bank: e.tensor_tensor(
                       bsl(x_tok, tc * 1024 + fg * 512, tc * 1024 + (fg + 1) * 512),
                       bsl(x_tok, tc * 1024 + fg * 512, tc * 1024 + (fg + 1) * 512),
                       ps_t[bank][:, 0:512], ALU.add), "wo")
        BA.free(wob); BA.free(otok); BA.free(orms); BA.free(ofeatc)
        FA.free(gwork); FA.free(sqw)
        BA.free(fs_tok); BA.free(fl_tok); BA.free(do_tok); BA.free(v_tok)
        BA.free(logf)
        FA.free(beta_t)

        GK = min(4, TCH)
        yq_c = BA.alloc("yq", 1024)
        t5f = FA.alloc("t5f", 1024)
        mbf = FA.alloc("mbf", 1024)
        c1ff = FA.alloc("c1ff", 1024)
        ffst = FA.alloc("ffst", TCH * 24)
        def emit_ffn_group(g0):
            tcs = list(range(g0, g0 + GK))
            yqf = BA.alloc("yqf", 8 * GK * 128)
            for tc in tcs:
                so = tc * 24
                ln_chunk(bsl(x_tok, tc * 1024, (tc + 1) * 1024),
                         bsl(x_tok, tc * 1024, tc * 1024 + 512),
                         bsl(x_tok, tc * 1024 + 512, (tc + 1) * 1024),
                         bsl(yq_c, 0, 1024), ffst, so, yq_c)
                OP("vector", [yq_c], [ffst],
                   lambda e, so=so: e.reduce_max(bsl(ffst, so + 17, so + 18),
                                                 bsl(yq_c, 0, 1024), AXX,
                                                 apply_absolute_value=True), "aq")
                OP("vector", [ffst], [ffst],
                   lambda e, so=so: e.tensor_scalar(bsl(ffst, so + 17, so + 18),
                                                    bsl(ffst, so + 17, so + 18),
                                                    1e-5, None, ALU.max), "aq")
                OP("vector", [ffst], [ffst],
                   lambda e, so=so: e.reciprocal(bsl(ffst, so + 18, so + 19),
                                                 bsl(ffst, so + 17, so + 18)), "aq")
                OP("vector", [ffst], [ffst],
                   lambda e, so=so: e.tensor_scalar(bsl(ffst, so + 18, so + 19),
                                                    bsl(ffst, so + 18, so + 19),
                                                    127.0, None, ALU.mult), "aq")
                OP("vector", [ffst, ffsc], [ffst],
                   lambda e, so=so: e.tensor_tensor(bsl(ffst, so + 19, so + 20),
                                                    bsl(ffst, so + 17, so + 18),
                                                    bsl(ffsc, 1, 2), ALU.mult), "aq")
                OP("vector", [ffst, ffsc], [ffst],
                   lambda e, so=so: e.tensor_tensor(bsl(ffst, so + 20, so + 21),
                                                    bsl(ffst, so + 17, so + 18),
                                                    bsl(ffsc, 3, 4), ALU.mult), "aq")
                OP("scalar", [yq_c, ffst], [t5f],
                   lambda e, so=so: e.activation(bsl(t5f, 0, 1024), bsl(yq_c, 0, 1024),
                                                 AF.Identity, bias=bsl(epsb, 2, 3),
                                                 scale=bsl(ffst, so + 18, so + 19)), "aq")
                emit_floor(t5f, bsl(t5f, 0, 1024), yq_c, bsl(yq_c, 0, 1024),
                           c1ff, bsl(c1ff, 0, 1024), mbf, bsl(mbf, 0, 1024), 1024)
                DMA("sync", bap(yqf, (tc - g0) * 128, [[GK * 128, 8], [1, 128]]),
                    bsl(yq_c, 0, 1024), [yq_c], [yqf], "yqT", transpose=True)
            z_tok = BA.alloc("z", GK * 4096)
            tslc = BA.alloc("tsl", GK * 512)
            wgus = [BA.alloc("wgu0", 2 * 4096), BA.alloc("wgu1", 2 * 4096)]
            for fgi in range(8):
                wgu = wgus[fgi % 2]
                for wi, wkey in enumerate(("g", "u")):
                    DMA("sync", bap(wgu, wi * 4096, [[512, 8], [1, 512]]),
                        bass.AP(s_w[wkey].tensor, fgi * 512,
                                [[INNER, 128], [128 * INNER, 8], [1, 512]]),
                        [dbuf(s_w[wkey])], [wgu], "wgu")
                for ti2, tc in enumerate(tcs):
                    bank = rotAD[ti2 % 4]
                    def em_g2(e, ti2=ti2, bank=bank, wgu=wgu):
                        r = None
                        for k in range(8):
                            r = e.matmul(ps_t[bank][:, 0:512],
                                         bsl(yqf, k * GK * 128 + ti2 * 128,
                                             k * GK * 128 + (ti2 + 1) * 128),
                                         bsl(wgu, k * 512, (k + 1) * 512),
                                         start=(k == 0), stop=(k == 7))
                        return r
                    OP("tensor", [yqf, wgu], [psb[bank]], em_g2, "ffg")
                    OP("scalar", [psb[bank], ffst], [tslc],
                       lambda e, ti2=ti2, tc=tc, bank=bank: e.activation(
                           bsl(tslc, ti2 * 512, (ti2 + 1) * 512), ps_t[bank][:, 0:512],
                           AF.Silu, bias=0.0,
                           scale=bsl(ffst, tc * 24 + 19, tc * 24 + 20)), "ffg")
                for ti2, tc in enumerate(tcs):
                    bank = rotAD[ti2 % 4]
                    def em_u2(e, ti2=ti2, bank=bank, wgu=wgu):
                        r = None
                        for k in range(8):
                            r = e.matmul(ps_t[bank][:, 0:512],
                                         bsl(yqf, k * GK * 128 + ti2 * 128,
                                             k * GK * 128 + (ti2 + 1) * 128),
                                         bsl(wgu, 4096 + k * 512, 4096 + (k + 1) * 512),
                                         start=(k == 0), stop=(k == 7))
                        return r
                    OP("tensor", [yqf, wgu], [psb[bank]], em_u2, "ffu")
                    OP("vector", [psb[bank], ffst, tslc], [z_tok],
                       lambda e, ti2=ti2, tc=tc, fgi=fgi, bank=bank: e.scalar_tensor_tensor(
                           bsl(z_tok, ti2 * 4096 + fgi * 512, ti2 * 4096 + (fgi + 1) * 512),
                           ps_t[bank][:, 0:512],
                           bsl(ffst, tc * 24 + 20, tc * 24 + 21),
                           bsl(tslc, ti2 * 512, (ti2 + 1) * 512), ALU.mult, ALU.mult), "ffu")
            BA.free(wgus[0]); BA.free(wgus[1]); BA.free(tslc); BA.free(yqf)
            zqf = BA.alloc("zqf", GK * 4096)
            wdbs = [BA.alloc("wd0", 512), BA.alloc("wd1", 512)]

            def emit_zq(ti2, tc):
                so = tc * 24
                OP("vector", [z_tok], [ffst],
                   lambda e, ti2=ti2, so=so: e.reduce_max(
                       bsl(ffst, so + 17, so + 18),
                       bsl(z_tok, ti2 * 4096, (ti2 + 1) * 4096), AXX,
                       apply_absolute_value=True), "zq")
                OP("vector", [ffst], [ffst],
                   lambda e, so=so: e.tensor_scalar(bsl(ffst, so + 17, so + 18),
                                                    bsl(ffst, so + 17, so + 18),
                                                    1e-5, None, ALU.max), "zq")
                OP("vector", [ffst], [ffst],
                   lambda e, so=so: e.reciprocal(bsl(ffst, so + 18, so + 19),
                                                 bsl(ffst, so + 17, so + 18)), "zq")
                OP("vector", [ffst], [ffst],
                   lambda e, so=so: e.tensor_scalar(bsl(ffst, so + 18, so + 19),
                                                    bsl(ffst, so + 18, so + 19),
                                                    127.0, None, ALU.mult), "zq")
                OP("vector", [ffst, ffsc], [ffst],
                   lambda e, so=so: e.tensor_tensor(bsl(ffst, so + 19, so + 20),
                                                    bsl(ffst, so + 17, so + 18),
                                                    bsl(ffsc, 5, 6), ALU.mult), "zq")
                for c0 in range(0, 4096, 1024):
                    OP("scalar", [z_tok, ffst, epsb], [t5f],
                       lambda e, ti2=ti2, c0=c0, so=so: e.activation(
                           bsl(t5f, 0, 1024),
                           bsl(z_tok, ti2 * 4096 + c0, ti2 * 4096 + c0 + 1024),
                           AF.Identity, bias=bsl(epsb, 2, 3),
                           scale=bsl(ffst, so + 18, so + 19)), "zq")
                    emit_floor(t5f, bsl(t5f, 0, 1024), z_tok,
                               bsl(z_tok, ti2 * 4096 + c0, ti2 * 4096 + c0 + 1024),
                               c1ff, bsl(c1ff, 0, 1024), mbf, bsl(mbf, 0, 1024), 1024)
                DMA("sync", bap(zqf, ti2 * 128, [[GK * 128, 32], [1, 128]]),
                    bsl(z_tok, ti2 * 4096, (ti2 + 1) * 4096), [z_tok], [zqf], "zqT",
                    transpose=True)

            def emit_down_pair(p0):
                tlo, thi = p0, p0 + 1
                for fg in range(2):
                    bklo = rotAD[fg * 2]
                    bkhi = rotAD[fg * 2 + 1]
                    for k in range(32):
                        wdb = wdbs[k % 2]
                        DMA("sync", bsl(wdb, 0, 512),
                            bass.AP(s_w["d"].tensor, k * 128 * D + fg * 512,
                                    [[D, 128], [1, 512]]),
                            [dbuf(s_w["d"])], [wdb], "wd")
                        def em_d(e, k=k, wdb=wdb, bklo=bklo, bkhi=bkhi, tlo=tlo, thi=thi):
                            e.matmul(ps_t[bklo][:, 0:512],
                                     bsl(zqf, k * GK * 128 + tlo * 128,
                                         k * GK * 128 + (tlo + 1) * 128),
                                     bsl(wdb, 0, 512), start=(k == 0), stop=(k == 31))
                            return e.matmul(ps_t[bkhi][:, 0:512],
                                            bsl(zqf, k * GK * 128 + thi * 128,
                                                k * GK * 128 + (thi + 1) * 128),
                                            bsl(wdb, 0, 512), start=(k == 0), stop=(k == 31))
                        OP("tensor", [zqf, wdb], [psb[bklo], psb[bkhi]], em_d, "ffd")
                    for off, bk in ((0, bklo), (1, bkhi)):
                        tc = g0 + p0 + off
                        OP("vector", [psb[bk], ffst, x_tok], [x_tok],
                           lambda e, tc=tc, fg=fg, bk=bk: e.scalar_tensor_tensor(
                               bsl(x_tok, tc * 1024 + fg * 512, tc * 1024 + (fg + 1) * 512),
                               ps_t[bk][:, 0:512],
                               bsl(ffst, tc * 24 + 19, tc * 24 + 20),
                               bsl(x_tok, tc * 1024 + fg * 512, tc * 1024 + (fg + 1) * 512),
                               ALU.mult, ALU.add), "ffd")

            for p0 in range(0, GK, 2):
                emit_zq(p0, g0 + p0)
                emit_zq(p0 + 1, g0 + p0 + 1)
                emit_down_pair(p0)
            BA.free(z_tok)
            BA.free(wdbs[0]); BA.free(wdbs[1]); BA.free(zqf)
        for g0 in range(0, TCH, GK):
            emit_ffn_group(g0)
        BA.free(yq_c); FA.free(t5f); FA.free(mbf); FA.free(c1ff); FA.free(ffst)
        FA.free(stats)

        for si in range(2):
            DMA("sync",
                bass.AP(out.tensor, (si * L + t0) * D, [[D, 128], [128 * D, TCS], [1, D]]),
                bap(x_tok, si * TCS * 1024, [[1024, TCS], [1, 1024]]),
                [x_tok], [dbuf(out)], "store")

    pre_phase()
    for ti in range(NTILE):
        emit_tile(ti)
    OP("sync", [dram_bufs["out"]], [], lambda e: e.nop(), "fence")

    P.finalize()
    with nc.Block() as block:
        for eng in ENGS:
            def run_stream(e, eng=eng):
                for it in P.items[eng]:
                    for w in it["waits"]:
                        e.wait_ge(sems[w[1]], P.wait_value(w))
                    if it["drain"]:
                        e.drain()
                    inst = it["emit"](e)
                    if it["dma"] is not None:
                        inst.then_inc(sems[it["dma"]], 16)
                    elif it["inc"]:
                        inst.then_inc(sems[eng], 1)
            getattr(block, eng)(run_stream)
    st.close()
    return nc


def kernel(**inputs):
    x = np.ascontiguousarray(inputs["L0_attn_hidden_states"], dtype=np.float32)
    B, L, _ = x.shape
    TPER = 512
    NTILE = L // TPER

    consts = host_consts()
    feed = {
        "W_q": inputs["L0_attn_W_q"], "W_k": inputs["L0_attn_W_k"],
        "W_v": inputs["L0_attn_W_v"], "W_b": inputs["L0_attn_W_b"],
        "conv_w_q": inputs["L0_attn_conv_w_q"], "conv_w_k": inputs["L0_attn_conv_w_k"],
        "conv_w_v": inputs["L0_attn_conv_w_v"],
        "fir_short_w": np.ascontiguousarray(inputs["L0_attn_fir_short_w"]).reshape(H * DV, 3),
        "fir_long_w": np.ascontiguousarray(inputs["L0_attn_fir_long_w"]).reshape(H * DV, 31),
        "W_fuse1": inputs["L0_attn_W_fuse1"], "b_fuse1": inputs["L0_attn_b_fuse1"],
        "W_fuse2": inputs["L0_attn_W_fuse2"], "b_fuse2": inputs["L0_attn_b_fuse2"],
        "others_log_tau": inputs["L0_attn_others_log_tau"],
        "norm_w": inputs["L0_attn_norm_w"], "W_o": inputs["L0_attn_W_o"],
        "w_gate": inputs["L0_ffn_w_gate"], "w_up": inputs["L0_ffn_w_up"],
        "w_down": inputs["L0_ffn_w_down"],
    }
    feed = {k: np.ascontiguousarray(v, dtype=np.float32) for k, v in feed.items()}
    for k, v in consts.items():
        feed["c_" + k] = v.astype(NPBF)
    feed["c_onesb"] = np.ones((1, 128), np.float32).astype(NPBF)
    feed["c_onesf_r"] = np.ones((1, 128), np.float32)
    feed["c_onesf_c"] = np.ones((128, 1), np.float32)

    nc = build(TPER=TPER, NTILE=NTILE)
    n = 8
    in_maps = []
    for c in range(n):
        m = dict(feed)
        m["x"] = np.ascontiguousarray(x[2 * c:2 * c + 2])
        in_maps.append(m)
    res = run_bass_kernel_spmd(nc, in_maps, core_ids=list(range(n)))
    return np.concatenate([r["out"] for r in res.results], axis=0)


# version 18
# speedup vs baseline: 1.1885x; 1.1885x over previous
import numpy as np
import ml_dtypes
from contextlib import ExitStack

import concourse.bass as bass
import concourse.mybir as mybir
from concourse.bass_utils import run_bass_kernel_spmd

BF16 = mybir.dt.bfloat16
F32 = mybir.dt.float32
AF = mybir.ActivationFunctionType
ALU = mybir.AluOpType
AXX = mybir.AxisListType.X
NPBF = ml_dtypes.bfloat16

H, DK, DV, D = 4, 256, 256, 1024
INNER, DHID = 4096, 2048
C = 128
ENGS = ("tensor", "vector", "scalar", "gpsimd", "sync")
DSEM = {"sync": "dS", "scalar": "dA", "gpsimd": "dG"}


def host_consts():
    i128 = np.eye(128, dtype=np.float32)
    strictL = np.tril(np.ones((128, 128), np.float32), -1)
    bd32 = np.zeros((128, 128), np.float32)
    for g in range(4):
        bd32[g*32:(g+1)*32, g*32:(g+1)*32] = 1.0
    mC32 = np.zeros((128, 128), np.float32)
    for g0 in range(0, 128, 64):
        mC32[g0:g0+32, g0+32:g0+64] = 1.0
    mC64 = np.zeros((128, 128), np.float32)
    mC64[0:64, 64:128] = 1.0
    return {
        "ident": i128,
        "mT_bd": -(strictL * bd32),
        "mTt": -strictL.T,
        "mTt_bd": -(strictL.T * bd32),
        "mloc": np.triu(np.ones((128, 128), np.float32), 0),
        "mC32": mC32, "mC64": mC64,
    }


class Buf:
    __slots__ = ("name", "last_w", "readers", "war", "arena", "off", "n")
    def __init__(self, name, war=()):
        self.name = name
        self.last_w = None
        self.readers = []
        self.war = list(war)


TAG_COST = {
    ("tensor", "proj"): 1800, ("tensor", "f1"): 7000, ("tensor", "f2"): 300,
    ("tensor", "ffg"): 1800, ("tensor", "ffu"): 1800, ("tensor", "ffd"): 500,
    ("tensor", "wo"): 1800, ("tensor", "gram"): 900, ("tensor", "loc"): 900,
    ("tensor", "inv"): 700, ("tensor", "mrg"): 700, ("tensor", "uw"): 900,
    ("tensor", "scan"): 500, ("tensor", "bc"): 700, ("tensor", "beta"): 3500,
    ("tensor", "T"): 1300, ("tensor", "fird"): 7800, ("tensor", "conv"): 1000,
    ("tensor", "wq"): 300, ("tensor", "rtau"): 300, ("tensor", "nw"): 300,
    ("vector", "fir"): 450, ("vector", "aq"): 1500, ("vector", "zq"): 1500,
    ("vector", "ffu"): 700, ("vector", "ffd"): 700, ("vector", "scan"): 600,
    ("vector", "ln"): 700, ("vector", "flr"): 800,
    ("scalar", "silu"): 1200, ("scalar", "f1"): 600, ("scalar", "ln"): 1100,
    ("scalar", "ffg"): 600, ("scalar", "wq"): 1100, ("scalar", "aq"): 1100,
    ("scalar", "zq"): 1100,
    ("gpsimd", "fir"): 1500,
}
DMA_COST = {
    "xload": 7000, "store": 7000, "wqkv": 7000, "w1": 4000, "wgu": 4000,
    "wd": 600, "wo": 7000, "cast": 16000, "wq": 2000, "c": 400,
    "hT": 1400, "kT": 1400, "vT": 1400, "doT": 1400, "fsT": 1400, "flT": 1400,
    "oT": 1400, "yqT": 1400, "zqT": 4500,
}


class Plan:
    def __init__(self):
        self.ops = []
        self.dma_nslots = {"dS": 12, "dA": 12, "dG": 6}

    def op(self, eng, emit, reads=(), writes=(), dma=None, tag=""):
        deps = set()
        for b in reads:
            if b.last_w is not None:
                deps.add(b.last_w)
        for b in writes:
            if b.last_w is not None:
                deps.add(b.last_w)
            deps.update(b.readers)
            deps.update(b.war)
        oid = len(self.ops)
        if dma is not None:
            cost = DMA_COST.get(tag, 2000)
        else:
            cost = TAG_COST.get((eng, tag), 500)
        self.ops.append({"eng": eng, "emit": emit, "deps": deps, "dma": dma,
                         "tag": tag, "cost": cost})
        for b in writes:
            b.last_w = oid
            b.readers = []
            b.war = []
        for b in reads:
            b.readers.append(oid)
        return oid

    def schedule(self):
        import heapq
        n = len(self.ops)
        out_edges = [[] for _ in range(n)]
        indeg = [0] * n
        for i, o in enumerate(self.ops):
            for d in o["deps"]:
                out_edges[d].append(i)
                indeg[i] += 1
        comp = [0.0] * n
        etime = {e: 0.0 for e in ENGS}
        qtime = {q: 0.0 for q in self.dma_nslots}
        ready = {e: [] for e in ENGS}
        for i, o in enumerate(self.ops):
            if indeg[i] == 0:
                heapq.heappush(ready[o["eng"]], (0.0, i))
        order = {e: [] for e in ENGS}
        scheduled = 0
        HOP = 400.0
        while scheduled < n:
            best = None
            for e in ENGS:
                while ready[e]:
                    est, i = ready[e][0]
                    break
                if not ready[e]:
                    continue
                est, i = ready[e][0]
                t = max(etime[e], est)
                if best is None or t < best[0]:
                    best = (t, e)
            t, e = best
            est, i = heapq.heappop(ready[e])
            o = self.ops[i]
            start = max(etime[e], est)
            if o["dma"] is not None:
                issue_end = start + 100.0
                etime[e] = issue_end
                done = max(qtime[o["dma"]], issue_end) + o["cost"]
                qtime[o["dma"]] = done
                comp[i] = done + HOP
            else:
                done = start + o["cost"]
                etime[e] = done
                comp[i] = done + HOP
            order[e].append(i)
            scheduled += 1
            for j in out_edges[i]:
                indeg[j] -= 1
                if indeg[j] == 0:
                    oj = self.ops[j]
                    est_j = max((comp[d] for d in oj["deps"]), default=0.0)
                    heapq.heappush(ready[oj["eng"]], (est_j, j))
        self.order = order

    def finalize(self):
        self.schedule()
        pos = {}
        for e in ENGS:
            for k, i in enumerate(self.order[e]):
                pos[i] = k
        self.pos = pos
        dma_ord = {}
        slot_count = {}
        self.dma_seminfo = {}
        slot_prev = {}
        self.dma_issuer_wait = {}
        qcount = {q: 0 for q in self.dma_nslots}
        for e in ENGS:
            for i in self.order[e]:
                o = self.ops[i]
                if o["dma"] is None:
                    continue
                q = o["dma"]
                sl = qcount[q] % self.dma_nslots[q]
                qcount[q] += 1
                semname = f"{q}{sl}"
                val = (slot_count.get(semname, 0) + 1) * 16
                slot_count[semname] = slot_count.get(semname, 0) + 1
                self.dma_seminfo[i] = (semname, val)
                if (q, sl) in slot_prev:
                    self.dma_issuer_wait[i] = self.dma_seminfo[slot_prev[(q, sl)]]
                slot_prev[(q, sl)] = i
        marks = {e: set() for e in ENGS}
        items = {e: [] for e in ENGS}
        for e in ENGS:
            waited = {}
            drain_upto = -1
            for k, i in enumerate(self.order[e]):
                o = self.ops[i]
                item = {"emit": o["emit"], "waits": [], "drain": False,
                        "inc": False, "dma": o["dma"], "id": i}
                need_drain = False
                for d in o["deps"]:
                    od = self.ops[d]
                    if od["dma"] is not None:
                        semname, val = self.dma_seminfo[d]
                        if waited.get(("dma", semname), -1) < val:
                            item["waits"].append(("dma", semname, val))
                            waited[("dma", semname)] = val
                    elif od["eng"] == e:
                        if e != "tensor" and pos[d] > drain_upto:
                            need_drain = True
                    else:
                        de = od["eng"]
                        if waited.get(("eng", de), -1) < pos[d]:
                            item["waits"].append(("eng", de, d))
                            waited[("eng", de)] = pos[d]
                            marks[de].add(d)
                if i in self.dma_issuer_wait:
                    semname, val = self.dma_issuer_wait[i]
                    if waited.get(("dma", semname), -1) < val:
                        item["waits"].append(("dma", semname, val))
                        waited[("dma", semname)] = val
                if need_drain:
                    item["drain"] = True
                    drain_upto = k - 1
                items[e].append(item)
        self.rank = {}
        for e in ENGS:
            marked = sorted(marks[e], key=lambda i: pos[i])
            r = {}
            for idx, i in enumerate(marked):
                r[i] = idx + 1
            self.rank[e] = r
            for item in items[e]:
                if item["id"] in r and item["dma"] is None:
                    item["inc"] = True
        self.items = items

    def wait_value(self, w):
        if w[0] == "dma":
            return w[2]
        return self.rank[w[1]][w[2]]
class Arena:
    def __init__(self, tensor, cols, name):
        self.t = tensor
        self.cols = cols
        self.name = name
        self.freelist = [(0, cols)]
        self.dead = []
        self.peak = 0
        self.used = 0

    def alloc(self, name, n):
        for i, (off, ln) in enumerate(self.freelist):
            if ln >= n:
                if ln == n:
                    self.freelist.pop(i)
                else:
                    self.freelist[i] = (off + n, ln - n)
                war, keep = [], []
                for (doff, dlen, toks) in self.dead:
                    if doff < off + n and off < doff + dlen:
                        war.extend(toks)
                        if doff < off:
                            keep.append((doff, off - doff, toks))
                        if doff + dlen > off + n:
                            keep.append((off + n, doff + dlen - (off + n), toks))
                    else:
                        keep.append((doff, dlen, toks))
                self.dead = keep
                b = Buf(f"{self.name}.{name}", war=war)
                b.arena = self
                b.off = off
                b.n = n
                self.used += n
                self.peak = max(self.peak, self.used)
                return b
        raise MemoryError(f"arena {self.name}: alloc {n} for {name} failed "
                          f"(used {self.used}/{self.cols})")

    def free(self, b):
        toks = list(b.readers)
        if b.last_w is not None:
            toks.append(b.last_w)
        self.dead.append((b.off, b.n, toks))
        self.used -= b.n
        self.freelist.append((b.off, b.n))
        self.freelist.sort()
        merged = []
        for off, ln in self.freelist:
            if merged and merged[-1][0] + merged[-1][1] == off:
                merged[-1] = (merged[-1][0], merged[-1][1] + ln)
            else:
                merged.append((off, ln))
        self.freelist = merged


def bsl(b, c0, c1, p0=0, p1=128):
    return b.arena.t[p0:p1, b.off + c0:b.off + c1]


def bap(b, col, dims, p0=0, pn=128):
    w = b.arena.t.shape[1]
    return bass.AP(b.arena.t, p0 * w + b.off + col, [[w, pn]] + dims)


def build(TPER=512, NTILE=4):
    L = TPER * NTILE
    TT = 2 * TPER
    TCH = TT // 128
    TCS = TCH // 2
    NCH = TPER // C
    VP = TPER + 30
    QP = TPER + 6
    TGW = min(512, TT)
    NTG = TT // TGW
    WD = NCH * 128

    nc = bass.Bass("TRN2", target_bir_lowering=False)
    st = ExitStack()

    def din(name, shape, dt=F32):
        return nc.dram_tensor(name, shape, dt, kind="ExternalInput").ap()

    x_in = din("x", [2, L, D])
    w_hbm = {"q": din("W_q", [D, D]), "k": din("W_k", [D, D]), "v": din("W_v", [D, D]),
             "o": din("W_o", [D, D]), "f1": din("W_fuse1", [INNER, DHID])}
    w_b = din("W_b", [D, H])
    cw = [din("conv_w_q", [D, 4]), din("conv_w_k", [D, 4]), din("conv_w_v", [D, 4])]
    fir_s_in = din("fir_short_w", [D, 3]); fir_l_in = din("fir_long_w", [D, 31])
    b_f1 = din("b_fuse1", [DHID])
    w_f2 = din("W_fuse2", [DHID, 16]); b_f2 = din("b_fuse2", [16])
    log_tau = din("others_log_tau", [H]); norm_w_in = din("norm_w", [DV])
    ffn_hbm = {"g": din("w_gate", [INNER, D]), "u": din("w_up", [INNER, D]),
               "d": din("w_down", [D, INNER])}
    CN = host_consts()
    cst_in = {k: din("c_" + k, [128, 128], BF16) for k in CN}
    onesb_in = din("c_onesb", [1, 128], BF16)
    onesf_r = din("c_onesf_r", [1, 128], F32)
    onesf_c = din("c_onesf_c", [128, 1], F32)
    out = nc.dram_tensor("out", [2, L, D], F32, kind="ExternalOutput").ap()

    def dsc(name, shape):
        return nc.dram_tensor(name, shape, BF16).ap()
    s_w = {"q": dsc("s_wq", [D, D]), "k": dsc("s_wk", [D, D]), "v": dsc("s_wv", [D, D]),
           "o": dsc("s_wo", [D, D]), "f1": dsc("s_w1", [INNER, DHID]),
           "g": dsc("s_wgT", [D, INNER]), "u": dsc("s_wuT", [D, INNER]),
           "d": dsc("s_wdT", [INNER, D])}

    FCOLS, BCOLS = 17600, 69120
    fa_t = st.enter_context(nc.sbuf_tensor("fa", [128, FCOLS], F32))
    ba_t = st.enter_context(nc.sbuf_tensor("ba", [128, BCOLS], BF16))
    int_t = st.enter_context(nc.sbuf_tensor("ib", [128, 1024], mybir.dt.int32))
    intb = Buf("intb")
    FA = Arena(fa_t, FCOLS, "f")
    BA = Arena(ba_t, BCOLS, "b")

    ps_t, psb = {}, {}
    for nm in "ABCDEFG":
        ps_t[nm] = st.enter_context(nc.psum_tensor("ps" + nm, [128, 512], F32))
        psb[nm] = Buf("ps" + nm)
    ps_t["T"] = st.enter_context(nc.psum_tensor("psT", [128, 1024], BF16))
    psb["T"] = Buf("psT")

    sems = {}
    for e in ENGS:
        sems[e] = st.enter_context(nc.semaphore("sem_" + e))
    for q, nsl in (("dS", 12), ("dA", 12), ("dG", 6)):
        for sl in range(nsl):
            sems[f"{q}{sl}"] = st.enter_context(nc.semaphore(f"{q}{sl}"))

    P = Plan()
    dram_bufs = {}
    def dbuf(ap):
        key = ap.tensor.name
        if key not in dram_bufs:
            dram_bufs[key] = Buf("hbm." + key)
        return dram_bufs[key]

    def OP(eng, reads, writes, emit, tag=""):
        return P.op(eng, emit, reads=reads, writes=writes, tag=tag)

    def DMA(q, dst, src, reads, writes, tag="", transpose=False, slow=False):
        if transpose:
            emit = lambda e, dd=dst, ss=src: e.dma_start_transpose(dd, ss)
        elif slow:
            emit = lambda e, dd=dst, ss=src: e.dma_start(dd, ss,
                                                         allow_slow_non_contiguous=True)
        else:
            emit = lambda e, dd=dst, ss=src: e.dma_start(dd, ss)
        return P.op(q, emit, reads=reads, writes=writes, dma=DSEM[q], tag=tag)

    rotAD = ["A", "B", "C", "D"]

    S_f = FA.alloc("S", 8 * 512)
    S_b = BA.alloc("Sb", 8 * 512)
    cstb = {k: BA.alloc("c_" + k, 128) for k in CN}
    onesb = BA.alloc("onesb", 128)
    onesf = FA.alloc("onesf", 129)
    convw = FA.alloc("convw", 3 * 32)
    firw = FA.alloc("firw", 8 * 34)
    b1b = FA.alloc("b1", 16)
    b2b = FA.alloc("b2", 1)
    rtau = FA.alloc("rtau", 16)
    nwrep = BA.alloc("nwrep", 256)
    w2b = BA.alloc("w2", 16 * 16)
    wbb = BA.alloc("wb", 8 * 4)
    qtails = BA.alloc("qtails", 3 * 8 * 2 * 4)
    vtails = BA.alloc("vtails", 8 * 2 * 30)
    ffsc = FA.alloc("ffsc", 8)
    epsb = FA.alloc("eps", 3)
    x_tok = FA.alloc("x", TCH * 1024)

    def cap(name, p0=0, p1=128, c0=0, c1=128):
        return bsl(cstb[name], c0, c1, p0, p1)

    def cap_rep(name, rep):
        return bap(cstb[name], 0, [[0, rep], [1, 128]])

    bgq = []
    qtb = {}

    def pump(n):
        while n > 0 and bgq:
            bgq.pop(0)()
            n -= 1

    def pump_flush():
        while bgq:
            bgq.pop(0)()
        if qtb:
            for b in qtb["acc"]:
                FA.free(b)
            for k in ("sc1", "t5", "mb"):
                FA.free(qtb[k])
            FA.free(qtb["cb"][0]); FA.free(qtb["cb"][1])
            for k in ("qb", "qbT"):
                BA.free(qtb[k][0]); BA.free(qtb[k][1])
            qtb.clear()

    def emit_floor(srcb, src_ap, dstb, dst_ap, c1b, c1_ap, gb, g_ap, width,
                   extra_r=(), pool=False):
        ge = "gpsimd" if pool else "vector"
        OP("vector", [srcb] + list(extra_r), [intb],
           lambda e, s=src_ap, w=width: e.tensor_copy(int_t[:, 0:w], s), "flr")
        OP("vector", [intb], [c1b],
           lambda e, c=c1_ap, w=width: e.tensor_copy(c, int_t[:, 0:w]), "flr")
        OP(ge, [srcb, c1b], [gb],
           lambda e, s=src_ap, c=c1_ap, g=g_ap: e.tensor_tensor(g, s, c, ALU.subtract), "flr")
        OP("vector", [gb, c1b], [c1b],
           lambda e, c=c1_ap, g=g_ap: e.scalar_tensor_tensor(
               c, g, 1.0, c, ALU.is_ge, ALU.add), "flr")
        OP("vector", [gb], [gb],
           lambda e, g=g_ap: e.tensor_scalar(g, g, 0.0, -1.0, ALU.is_ge, ALU.add), "flr")
        OP(ge, [c1b, gb], [dstb],
           lambda e, c=c1_ap, g=g_ap, d=dst_ap: e.tensor_tensor(d, c, g, ALU.add), "flr")

    def pre_phase():
        OP("vector", [], [epsb], lambda e: e.memset(bsl(epsb, 0, 1), 1e-5), "c")
        OP("vector", [], [epsb], lambda e: e.memset(bsl(epsb, 1, 2), 1e-6), "c")
        OP("vector", [], [epsb], lambda e: e.memset(bsl(epsb, 2, 3), 0.5), "c")
        for kk in CN:
            DMA("sync", cap(kk), cst_in[kk][:, :], [dbuf(cst_in[kk])], [cstb[kk]], "c")
        DMA("sync", bsl(onesb, 0, 128, 0, 1), onesb_in[:, :], [dbuf(onesb_in)], [onesb], "c")
        DMA("sync", bsl(onesf, 0, 128, 0, 1), onesf_r[:, :], [dbuf(onesf_r)], [onesf], "c")
        DMA("sync", bsl(onesf, 128, 129), onesf_c[:, :], [dbuf(onesf_c)], [onesf], "c")
        for pi in range(3):
            DMA("sync", bap(convw, pi * 32, [[4, 8], [1, 4]]),
                bass.AP(cw[pi].tensor, 0, [[4, 128], [512, 8], [1, 4]]),
                [dbuf(cw[pi])], [convw], "c")
        DMA("sync", bap(firw, 0, [[34, 8], [1, 31]]),
            bass.AP(fir_l_in.tensor, 0, [[31, 128], [31 * 128, 8], [1, 31]]),
            [dbuf(fir_l_in)], [firw], "c")
        DMA("sync", bap(firw, 31, [[34, 8], [1, 3]]),
            bass.AP(fir_s_in.tensor, 0, [[3, 128], [3 * 128, 8], [1, 3]]),
            [dbuf(fir_s_in)], [firw], "c")
        DMA("sync", bap(b1b, 0, [[1, 16]]),
            bass.AP(b_f1.tensor, 0, [[1, 128], [128, 16]]), [dbuf(b_f1)], [b1b], "c",
            slow=True)
        DMA("sync", bsl(b2b, 0, 1, 0, 16), bass.AP(b_f2.tensor, 0, [[1, 16], [1, 1]]),
            [dbuf(b_f2)], [b2b], "c")
        DMA("gpsimd", bap(w2b, 0, [[16, 16], [1, 16]]),
            bass.AP(w_f2.tensor, 0, [[16, 128], [16 * 128, 16], [1, 16]]),
            [dbuf(w_f2)], [w2b], "c")
        DMA("gpsimd", bap(wbb, 0, [[4, 8], [1, 4]]),
            bass.AP(w_b.tensor, 0, [[4, 128], [4 * 128, 8], [1, 4]]),
            [dbuf(w_b)], [wbb], "c")

        ltb = FA.alloc("lt", 8)
        DMA("sync", bsl(ltb, 0, 4, 0, 1), bass.AP(log_tau.tensor, 0, [[4, 1], [1, 4]]),
            [dbuf(log_tau)], [ltb], "c")
        OP("scalar", [ltb], [ltb],
           lambda e: e.activation(bsl(ltb, 4, 8, 0, 1), bsl(ltb, 0, 4, 0, 1), AF.Exp,
                                  bias=0.0, scale=-1.0), "rtau")
        OP("tensor", [ltb, onesf], [psb["A"]],
           lambda e: e.matmul(ps_t["A"][:, 0:4], bsl(onesf, 0, 128, 0, 1),
                              bsl(ltb, 4, 8, 0, 1), start=True, stop=True), "rtau")
        OP("vector", [], [rtau],
           lambda e: e.memset(bap(rtau, 3, [[4, 4]]), 1.0), "rtau")
        for j in range(3):
            OP("vector", [psb["A"]], [rtau],
               lambda e, j=j: e.tensor_copy(bap(rtau, j, [[4, 4]]), ps_t["A"][:, 0:4]), "rtau")
        FA.free(ltb)

        nw1 = FA.alloc("nw1", 256)
        nw1b = BA.alloc("nw1b", 256)
        DMA("sync", bsl(nw1, 0, 256, 0, 1),
            bass.AP(norm_w_in.tensor, 0, [[256, 1], [1, 256]]),
            [dbuf(norm_w_in)], [nw1], "c")
        OP("vector", [nw1], [nw1b],
           lambda e: e.tensor_copy(bsl(nw1b, 0, 256, 0, 1), bsl(nw1, 0, 256, 0, 1)), "nw")
        OP("tensor", [nw1b, onesb], [psb["C"]],
           lambda e: e.matmul(ps_t["C"][:, 0:256], bsl(onesb, 0, 128, 0, 1),
                              bsl(nw1b, 0, 256, 0, 1), start=True, stop=True), "nw")
        OP("vector", [psb["C"]], [nwrep],
           lambda e: e.tensor_copy(bsl(nwrep, 0, 256), ps_t["C"][:, 0:256]), "nw")
        FA.free(nw1); BA.free(nw1b)

        OP("vector", [], [S_f], lambda e: e.memset(bsl(S_f, 0, 8 * 512), 0.0), "init")
        OP("vector", [], [S_b], lambda e: e.memset(bsl(S_b, 0, 8 * 512), 0.0), "init")
        OP("vector", [], [qtails], lambda e: e.memset(bsl(qtails, 0, 3 * 8 * 2 * 4), 0.0), "init")
        OP("vector", [], [vtails], lambda e: e.memset(bsl(vtails, 0, 8 * 2 * 30), 0.0), "init")

        qtb["acc"] = [FA.alloc(f"qt_acc{i}", 2) for i in range(3)]
        qtb["sc1"] = FA.alloc("qt_sc1", 4)
        qtb["cb"] = [FA.alloc("qt_cb0", 1024), FA.alloc("qt_cb1", 1024)]
        qtb["t5"] = FA.alloc("qt_t5", 1024)
        qtb["mb"] = FA.alloc("qt_mb", 1024)
        qtb["qb"] = [BA.alloc("qt_qb0", 1024), BA.alloc("qt_qb1", 1024)]
        qtb["qbT"] = [BA.alloc("qt_qbT0", 1024), BA.alloc("qt_qbT1", 1024)]
        stage = BA.alloc("stage", 8192)
        def cast_big(key, rows, cols):
            src, scr = w_hbm[key], s_w[key]
            rows_per = min(max(1, 8192 // cols) * 128, rows)
            nk = rows_per // 128
            for r0 in range(0, rows, rows_per):
                DMA("gpsimd", bap(stage, 0, [[cols, nk], [1, cols]]),
                    bass.AP(src.tensor, r0 * cols, [[cols, 128], [128 * cols, nk], [1, cols]]),
                    [dbuf(src)], [stage], "cast")
                DMA("sync",
                    bass.AP(scr.tensor, r0 * cols, [[cols, 128], [128 * cols, nk], [1, cols]]),
                    bap(stage, 0, [[cols, nk], [1, cols]]),
                    [stage], [dbuf(scr)], "cast")
        for key in ("q", "k", "v", "o"):
            cast_big(key, D, D)
        cast_big("f1", INNER, DHID)
        BA.free(stage)

        sc1 = qtb["sc1"]
        t5, mb = qtb["t5"], qtb["mb"]
        wq_cfg = ((0, "g", D, INNER), (1, "u", D, INNER), (2, "d", INNER, D))

        def u_p1(scol, key, KIN, NOUT, kc2):
            src, acc = ffn_hbm[key], qtb["acc"][scol]
            WIDTH = KIN
            SUBW = min(WIDTH, 1024)
            NSUB = WIDTH // SUBW
            kc, sub = kc2 // NSUB, kc2 % NSUB
            cb = qtb["cb"][kc2 % 2]
            if kc2 == 0:
                OP("vector", [], [acc], lambda e, a=acc: e.memset(bsl(a, 0, 1), 0.0), "wq")
            DMA("sync", bsl(cb, 0, SUBW),
                bass.AP(src.tensor, kc * 128 * WIDTH + sub * SUBW,
                        [[WIDTH, 128], [1, SUBW]]),
                [dbuf(src)], [cb], "wq")
            OP("scalar", [cb], [t5, acc],
               lambda e, cb=cb, a=acc, S=SUBW: e.activation(
                   bsl(t5, 0, S), bsl(cb, 0, S), AF.Abs, bias=0.0, scale=1.0,
                   accum_out=bsl(a, 1, 2)), "wq")
            OP("vector", [acc], [acc],
               lambda e, a=acc: e.tensor_tensor(bsl(a, 0, 1), bsl(a, 0, 1),
                                                bsl(a, 1, 2), ALU.add), "wq")

        def u_fin(scol, key, KIN, NOUT):
            acc = qtb["acc"][scol]
            OP("tensor", [acc, onesf], [psb["A"]],
               lambda e, a=acc: e.matmul(ps_t["A"][0:1, 0:1], bsl(onesf, 128, 129),
                                         bsl(a, 0, 1), start=True, stop=True), "wq")
            OP("vector", [psb["A"]], [sc1],
               lambda e, nn=1.0 / (KIN * NOUT): e.tensor_scalar(
                   bsl(sc1, 0, 1, 0, 1), ps_t["A"][0:1, 0:1], nn, 1e-5,
                   ALU.mult, ALU.max), "wq")
            OP("vector", [sc1], [sc1],
               lambda e: e.reciprocal(bsl(sc1, 1, 2, 0, 1), bsl(sc1, 0, 1, 0, 1)), "wq")
            OP("vector", [sc1], [sc1],
               lambda e: e.tensor_scalar(bsl(sc1, 2, 3, 0, 1), bsl(sc1, 0, 1, 0, 1),
                                         1.0 / 127.0, None, ALU.mult), "wq")
            OP("tensor", [sc1, onesf], [psb["B"]],
               lambda e: e.matmul(ps_t["B"][:, 0:2], bsl(onesf, 0, 128, 0, 1),
                                  bsl(sc1, 1, 3, 0, 1), start=True, stop=True), "wq")
            OP("vector", [psb["B"]], [ffsc],
               lambda e, sc=scol: e.tensor_copy(bsl(ffsc, sc * 2, sc * 2 + 2),
                                                ps_t["B"][:, 0:2]), "wq")

        def u_p2(scol, key, KIN, NOUT, kc2):
            src, scr = ffn_hbm[key], s_w[key]
            WIDTH = KIN
            SUBW = min(WIDTH, 1024)
            NSUB = WIDTH // SUBW
            nic = SUBW // 128
            kc, sub = kc2 // NSUB, kc2 % NSUB
            cb = qtb["cb"][kc2 % 2]
            qb = qtb["qb"][kc2 % 2]
            qbT = qtb["qbT"][kc2 % 2]
            DMA("sync", bsl(cb, 0, SUBW),
                bass.AP(src.tensor, kc * 128 * WIDTH + sub * SUBW,
                        [[WIDTH, 128], [1, SUBW]]),
                [dbuf(src)], [cb], "wq")
            OP("scalar", [cb, ffsc], [t5],
               lambda e, cb=cb, sc=scol: e.activation(
                   bsl(t5, 0, 1024), bsl(cb, 0, 1024), AF.Abs,
                   bias=0.0, scale=bsl(ffsc, sc * 2, sc * 2 + 1)), "wq")
            OP("scalar", [cb], [mb],
               lambda e, cb=cb: e.activation(bsl(mb, 0, 1024), bsl(cb, 0, 1024),
                                             AF.Sign, bias=0.0, scale=1.0), "wq")
            OP("vector", [t5], [t5],
               lambda e: e.tensor_scalar(bsl(t5, 0, 1024), bsl(t5, 0, 1024),
                                         0.5, None, ALU.is_ge), "wq")
            OP("vector", [t5, mb], [qb],
               lambda e, qb=qb: e.tensor_tensor(bsl(qb, 0, 1024), bsl(t5, 0, 1024),
                                                bsl(mb, 0, 1024), ALU.mult), "wq")
            DMA("sync", bap(qbT, 0, [[128, nic], [1, 128]]),
                bsl(qb, 0, SUBW), [qb], [qbT], "wq", transpose=True)
            DMA("sync",
                bass.AP(scr.tensor, (sub * SUBW) * NOUT + kc * 128,
                        [[NOUT, 128], [128 * NOUT, nic], [1, 128]]),
                bap(qbT, 0, [[128, nic], [1, 128]]),
                [qbT], [dbuf(scr)], "wq")

        for scol, key, KIN, NOUT in wq_cfg:
            n_units = (NOUT // 128) * (KIN // min(KIN, 1024))
            for kc2 in range(n_units):
                bgq.append(lambda s=scol, k=key, a=KIN, b=NOUT, c=kc2: u_p1(s, k, a, b, c))
        for scol, key, KIN, NOUT in wq_cfg:
            bgq.append(lambda s=scol, k=key, a=KIN, b=NOUT: u_fin(s, k, a, b))
        for scol, key, KIN, NOUT in wq_cfg:
            n_units = (NOUT // 128) * (KIN // min(KIN, 1024))
            for kc2 in range(n_units):
                bgq.append(lambda s=scol, k=key, a=KIN, b=NOUT, c=kc2: u_p2(s, k, a, b, c))

    def ln_chunk(x_ap, x_ap_lo, x_ap_hi, out_ap, stats, scol, out_buf):
        OP("vector", [x_tok], [stats],
           lambda e, a=x_ap_lo, s6=bsl(stats, scol, scol + 6): e.bn_stats(s6, a), "ln")
        OP("vector", [x_tok], [stats],
           lambda e, a=x_ap_hi, s6=bsl(stats, scol + 6, scol + 12): e.bn_stats(s6, a), "ln")
        OP("vector", [stats], [stats],
           lambda e, s12=bsl(stats, scol, scol + 12),
                  mv=bsl(stats, scol + 12, scol + 14): e.bn_aggr(mv, s12), "ln")
        OP("scalar", [stats, epsb], [stats],
           lambda e, i=bsl(stats, scol + 13, scol + 14),
                  o=bsl(stats, scol + 14, scol + 15): e.activation(
                      o, i, AF.Sqrt, bias=bsl(epsb, 0, 1), scale=1.0), "ln")
        OP("vector", [stats], [stats],
           lambda e, i=bsl(stats, scol + 14, scol + 15),
                  o=bsl(stats, scol + 15, scol + 16): e.reciprocal(o, i), "ln")
        OP("vector", [stats], [stats],
           lambda e, m=bsl(stats, scol + 12, scol + 13),
                  r=bsl(stats, scol + 15, scol + 16),
                  o=bsl(stats, scol + 16, scol + 17): e.scalar_tensor_tensor(
                      o, m, -1.0, r, ALU.mult, ALU.mult), "ln")
        OP("scalar", [x_tok, stats], [out_buf],
           lambda e, a=x_ap, o=out_ap,
                  bi=bsl(stats, scol + 16, scol + 17),
                  sc=bsl(stats, scol + 15, scol + 16): e.activation(
                      o, a, AF.Identity, bias=bi, scale=sc), "ln")

    def emit_h(h_feat, stats, apply_only=False):
        htokb = [BA.alloc("htok0", 1024), BA.alloc("htok1", 1024)]
        for tc in range(TCH):
            hb = htokb[tc % 2]
            if apply_only:
                scol = tc * 24
                OP("scalar", [x_tok, stats], [hb],
                   lambda e, a=bsl(x_tok, tc * 1024, (tc + 1) * 1024),
                          o=bsl(hb, 0, 1024),
                          bi=bsl(stats, scol + 16, scol + 17),
                          sc=bsl(stats, scol + 15, scol + 16): e.activation(
                              o, a, AF.Identity, bias=bi, scale=sc), "ln")
            else:
                ln_chunk(bsl(x_tok, tc * 1024, (tc + 1) * 1024),
                         bsl(x_tok, tc * 1024, tc * 1024 + 512),
                         bsl(x_tok, tc * 1024 + 512, (tc + 1) * 1024),
                         bsl(hb, 0, 1024), stats, tc * 24, hb)
            DMA("sync", bap(h_feat, tc * 128, [[TT, 8], [1, 128]]),
                bsl(hb, 0, 1024), [hb], [h_feat], "hT", transpose=True)
        BA.free(htokb[0]); BA.free(htokb[1])

    def make_prep_segments(si, hi, bufs, segs):
        (q_feat, k_feat, k_tok, v_tok, do_tok, beta_t, dwork,
         Gsb, t1w, t2w, Tbd, Ttf, Tbdt, locT, Aw, Bw, Pw, Ptw,
         u_b, what, kbw, vbw, uadj, rsp, rowT, RSJ, BRSJ, BRSJb) = bufs

        def qk_ap(buf, half, n):
            base = (2 * hi + half) * TT + si * TPER + n * 128
            return bsl(buf, base, base + 128)

        def tok_sl(buf, n, w=256):
            base = (si * TCS + n) * 1024 + hi * 256
            return bsl(buf, base, base + w)

        rs_k = lambda n: bsl(dwork, WD + n, WD + n + 1)
        brs = lambda n: bsl(dwork, WD + 2 * NCH + n, WD + 2 * NCH + n + 1)
        bcol = lambda n: bsl(beta_t, (si * TCS + n) * 4 + hi, (si * TCS + n) * 4 + hi + 1)

        def st_gram():
            for which, fbuf, ro in (("k", k_feat, 0), ("q", q_feat, NCH)):
                def em_g(e, fbuf=fbuf):
                    r = None
                    for n in range(NCH):
                        for half in range(2):
                            r = e.matmul(ps_t["A"][:, n * 128:(n + 1) * 128],
                                         qk_ap(fbuf, half, n), qk_ap(fbuf, half, n),
                                         start=(half == 0), stop=(half == 1))
                    return r
                OP("tensor", [fbuf], [psb["A"]], em_g, "gram")
                if which == "k":
                    OP("vector", [psb["A"]], [Gsb],
                       lambda e: e.tensor_copy(bsl(Gsb, 0, WD), ps_t["A"][:, 0:WD]), "gram")
                OP("vector", [psb["A"], cstb["ident"]], [dwork],
                   lambda e: e.tensor_tensor(bsl(dwork, 0, WD), ps_t["A"][:, 0:WD],
                                             cap_rep("ident", NCH), ALU.mult), "rs")
                OP("vector", [dwork], [dwork],
                   lambda e, ro=ro: e.reduce_sum(
                       bap(dwork, WD + 3 * NCH + ro, [[1, NCH], [1, 1]]),
                       bap(dwork, 0, [[128, NCH], [1, 128]]), AXX), "rs")
                OP("scalar", [dwork, epsb], [dwork],
                   lambda e, ro=ro: e.activation(
                       bsl(dwork, WD + 5 * NCH + ro, WD + 5 * NCH + ro + NCH),
                       bsl(dwork, WD + 3 * NCH + ro, WD + 3 * NCH + ro + NCH),
                       AF.Sqrt, bias=bsl(epsb, 1, 2), scale=1.0), "rs")
                OP("vector", [dwork], [dwork],
                   lambda e, ro=ro: e.reciprocal(
                       bsl(dwork, WD + ro, WD + ro + NCH),
                       bsl(dwork, WD + 5 * NCH + ro, WD + 5 * NCH + ro + NCH)), "rs")
            OP("vector", [beta_t, dwork], [dwork],
               lambda e: e.tensor_tensor(
                   bsl(dwork, WD + 2 * NCH, WD + 3 * NCH),
                   bap(beta_t, (si * TCS) * 4 + hi, [[4, NCH]]),
                   bsl(dwork, WD, WD + NCH), ALU.mult), "rs")
        segs.append(st_gram)

        def st_ktn():
            for n in range(NCH):
                OP("vector", [k_tok, dwork], [k_tok],
                   lambda e, n=n: e.tensor_scalar(tok_sl(k_tok, n), tok_sl(k_tok, n),
                                                  rs_k(n), None, ALU.mult), "ktn")
                OP("vector", [k_tok, beta_t, dwork], [kbw],
                   lambda e, n=n: e.tensor_scalar(bsl(kbw, n * 256, (n + 1) * 256),
                                                  tok_sl(k_tok, n), bcol(n), None, ALU.mult), "ktn")
                OP("vector", [v_tok, beta_t], [vbw],
                   lambda e, n=n: e.tensor_scalar(bsl(vbw, n * 256, (n + 1) * 256),
                                                  tok_sl(v_tok, n), bcol(n), None, ALU.mult), "ktn")
        segs.append(st_ktn)

        def st_bc():
            OP("vector", [dwork], [rsp],
               lambda e: e.tensor_copy(bsl(rsp, 0, NCH), bsl(dwork, WD, WD + NCH)), "bc")
            OP("vector", [dwork], [rsp],
               lambda e: e.tensor_copy(bsl(rsp, NCH, 2 * NCH),
                                       bsl(dwork, WD + 2 * NCH, WD + 3 * NCH)), "bc")
            OP("tensor", [rsp, cstb["ident"]], [psb["T"]],
               lambda e: e.transpose(ps_t["T"][0:2 * NCH, 0:128],
                                     bsl(rsp, 0, 2 * NCH), cap("ident")), "bc")
            OP("vector", [psb["T"]], [rowT],
               lambda e: e.tensor_copy(bsl(rowT, 0, 128, 0, 2 * NCH),
                                       ps_t["T"][0:2 * NCH, 0:128]), "bc")
            def sel_ap(n):
                return bap(cstb["ident"], n, [[0, 128]], 0, 2 * NCH)
            def em_bc(e):
                r = None
                for n in range(NCH):
                    r = e.matmul(ps_t["D"][:, n * 128:(n + 1) * 128], sel_ap(n),
                                 bsl(rowT, 0, 128, 0, 2 * NCH), start=True, stop=True)
                return r
            OP("tensor", [rowT, cstb["ident"]], [psb["D"]], em_bc, "bc")
            OP("vector", [psb["D"], cstb["mT_bd"]], [RSJ],
               lambda e: e.tensor_tensor(bsl(RSJ, 0, WD), ps_t["D"][:, 0:WD],
                                         cap_rep("mT_bd", NCH), ALU.mult), "bc")
            def em_bc2(e):
                r = None
                for n in range(NCH):
                    r = e.matmul(ps_t["D"][:, n * 128:(n + 1) * 128], sel_ap(NCH + n),
                                 bsl(rowT, 0, 128, 0, 2 * NCH), start=True, stop=True)
                return r
            OP("tensor", [rowT, cstb["ident"]], [psb["D"]], em_bc2, "bc")
            OP("vector", [psb["D"], cstb["mTt"]], [BRSJ],
               lambda e: e.tensor_tensor(bsl(BRSJ, 0, WD), ps_t["D"][:, 0:WD],
                                         cap_rep("mTt", NCH), ALU.mult), "bc")
        segs.append(st_bc)

        def st_T():
            for n in range(NCH):
                OP("vector", [Gsb, dwork], [t1w],
                   lambda e, n=n: e.tensor_scalar(bsl(t1w, n * 128, (n + 1) * 128),
                                                  bsl(Gsb, n * 128, (n + 1) * 128),
                                                  brs(n), None, ALU.mult), "T")
                OP("vector", [Gsb, dwork], [t2w],
                   lambda e, n=n: e.tensor_scalar(bsl(t2w, n * 128, (n + 1) * 128),
                                                  bsl(Gsb, n * 128, (n + 1) * 128),
                                                  rs_k(n), None, ALU.mult), "T")
            OP("vector", [t1w, RSJ], [Tbd],
               lambda e: e.tensor_tensor(bsl(Tbd, 0, WD), bsl(t1w, 0, WD),
                                         bsl(RSJ, 0, WD), ALU.mult), "T")
            OP("vector", [t2w, BRSJ], [Ttf],
               lambda e: e.tensor_tensor(bsl(Ttf, 0, WD), bsl(t2w, 0, WD),
                                         bsl(BRSJ, 0, WD), ALU.mult), "T")
            def em_tbdt(e):
                r = None
                for n in range(NCH):
                    r = e.transpose(ps_t["T"][:, n * 128:(n + 1) * 128],
                                    bsl(Tbd, n * 128, (n + 1) * 128), cap("ident"))
                return r
            OP("tensor", [Tbd, cstb["ident"]], [psb["T"]], em_tbdt, "T")
            OP("vector", [psb["T"]], [Tbdt],
               lambda e: e.tensor_copy(bsl(Tbdt, 0, WD), ps_t["T"][:, 0:WD]), "T")
        segs.append(st_T)

        def st_loc():
            def em_loc(e):
                r = None
                for n in range(NCH):
                    for half in range(2):
                        r = e.matmul(ps_t["A"][:, n * 128:(n + 1) * 128],
                                     qk_ap(k_feat, half, n), qk_ap(q_feat, half, n),
                                     start=(half == 0), stop=(half == 1))
                return r
            OP("tensor", [k_feat, q_feat], [psb["A"]], em_loc, "loc")
            for n in range(NCH):
                OP("vector", [psb["A"], dwork], [t1w],
                   lambda e, n=n: e.tensor_scalar(bsl(t1w, n * 128, (n + 1) * 128),
                                                  ps_t["A"][:, n * 128:(n + 1) * 128],
                                                  rs_k(n), None, ALU.mult), "loc")
            OP("vector", [t1w, cstb["mloc"]], [locT],
               lambda e: e.tensor_tensor(bsl(locT, 0, WD), bsl(t1w, 0, WD),
                                         cap_rep("mloc", NCH), ALU.mult), "loc")
        segs.append(st_loc)

        def st_ainit():
            OP("vector", [Tbd, cstb["ident"]], [Aw],
               lambda e: e.tensor_tensor(bsl(Aw, 0, WD), bsl(Tbd, 0, WD),
                                         cap_rep("ident", NCH), ALU.add), "inv")
        segs.append(st_ainit)

        chain = [(Tbd, Tbdt), (Pw, Ptw)]
        for lev in range(4):
            Pcur, Ptcur = chain[lev % 2]
            Pnxt, Ptnxt = chain[(lev + 1) % 2]
            last = (lev == 3)
            def st_lev(Pcur=Pcur, Ptcur=Ptcur, Pnxt=Pnxt, Ptnxt=Ptnxt, last=last):
                if not last:
                    def em_p2(e):
                        r = None
                        for n in range(NCH):
                            r = e.matmul(ps_t["A"][:, n * 128:(n + 1) * 128],
                                         bsl(Ptcur, n * 128, (n + 1) * 128),
                                         bsl(Pcur, n * 128, (n + 1) * 128),
                                         start=True, stop=True)
                        return r
                    OP("tensor", [Pcur, Ptcur], [psb["A"]], em_p2, "inv")
                def em_p2t(e):
                    r = None
                    for n in range(NCH):
                        r = e.matmul(ps_t["B"][:, n * 128:(n + 1) * 128],
                                     bsl(Pcur, n * 128, (n + 1) * 128),
                                     bsl(Ptcur, n * 128, (n + 1) * 128),
                                     start=True, stop=True)
                    return r
                OP("tensor", [Pcur, Ptcur], [psb["B"]], em_p2t, "inv")
                if not last:
                    OP("vector", [psb["A"]], [Pnxt],
                       lambda e: e.tensor_copy(bsl(Pnxt, 0, WD), ps_t["A"][:, 0:WD]), "inv")
                OP("vector", [psb["B"]], [Ptnxt],
                   lambda e: e.tensor_copy(bsl(Ptnxt, 0, WD), ps_t["B"][:, 0:WD]), "inv")
                def em_aup(e):
                    r = None
                    for n in range(NCH):
                        r = e.matmul(ps_t["E"][:, n * 128:(n + 1) * 128],
                                     bsl(Ptnxt, n * 128, (n + 1) * 128),
                                     bsl(Aw, n * 128, (n + 1) * 128), start=True, stop=True)
                    return r
                OP("tensor", [Ptnxt, Aw], [psb["E"]], em_aup, "inv")
                OP("vector", [psb["E"], Aw], [Aw],
                   lambda e: e.tensor_tensor(bsl(Aw, 0, WD), bsl(Aw, 0, WD),
                                             ps_t["E"][:, 0:WD], ALU.add), "inv")
            segs.append(st_lev)

        def st_b32():
            def em_b32(e):
                r = None
                for n in range(NCH):
                    r = e.transpose(ps_t["T"][:, n * 128:(n + 1) * 128],
                                    bsl(Aw, n * 128, (n + 1) * 128), cap("ident"))
                return r
            OP("tensor", [Aw, cstb["ident"]], [psb["T"]], em_b32, "inv")
            OP("vector", [psb["T"]], [Bw],
               lambda e: e.tensor_copy(bsl(Bw, 0, WD), ps_t["T"][:, 0:WD]), "inv")
        segs.append(st_b32)

        Ctw, Yw, Xw = Gsb, t1w, t2w
        for mname, paste in (("mC32", [(32, 64, 0), (96, 128, 64)]),
                             ("mC64", [(64, 128, 0)])):
            def st_mrg(mname=mname, paste=paste):
                OP("vector", [Ttf, cstb[mname]], [Ctw],
                   lambda e: e.tensor_tensor(bsl(Ctw, 0, WD), bsl(Ttf, 0, WD),
                                             cap_rep(mname, NCH), ALU.mult), "mrg")
                def em_y(e):
                    r = None
                    for n in range(NCH):
                        r = e.matmul(ps_t["A"][:, n * 128:(n + 1) * 128],
                                     bsl(Ctw, n * 128, (n + 1) * 128),
                                     bsl(Aw, n * 128, (n + 1) * 128), start=True, stop=True)
                    return r
                OP("tensor", [Ctw, Aw], [psb["A"]], em_y, "mrg")
                OP("vector", [psb["A"]], [Yw],
                   lambda e: e.tensor_copy(bsl(Yw, 0, WD), ps_t["A"][:, 0:WD]), "mrg")
                def em_x(e):
                    r = None
                    for n in range(NCH):
                        r = e.matmul(ps_t["B"][:, n * 128:(n + 1) * 128],
                                     bsl(Bw, n * 128, (n + 1) * 128),
                                     bsl(Yw, n * 128, (n + 1) * 128), start=True, stop=True)
                    return r
                OP("tensor", [Bw, Yw], [psb["B"]], em_x, "mrg")
                OP("vector", [psb["B"]], [Xw],
                   lambda e: e.tensor_copy(bsl(Xw, 0, WD), ps_t["B"][:, 0:WD]), "mrg")
                for (r0, r1, c0) in paste:
                    wdt = r1 - r0
                    OP("vector", [psb["B"], Aw], [Aw],
                       lambda e, r0=r0, r1=r1, c0=c0, wdt=wdt: e.tensor_tensor(
                           bap(Aw, c0, [[128, NCH], [1, wdt]], p0=r0, pn=r1 - r0),
                           bap(Aw, c0, [[128, NCH], [1, wdt]], p0=r0, pn=r1 - r0),
                           bass.AP(ps_t["B"], r0 * 512 + c0, [[512, wdt], [128, NCH], [1, wdt]]),
                           ALU.add), "mrg")
                def em_xt(e):
                    r = None
                    for n in range(NCH):
                        r = e.transpose(ps_t["T"][:, n * 128:(n + 1) * 128],
                                        bsl(Xw, n * 128, (n + 1) * 128), cap("ident"))
                    return r
                OP("tensor", [Xw, cstb["ident"]], [psb["T"]], em_xt, "mrg")
                for (r0, r1, c0) in paste:
                    wdt = r1 - r0
                    OP("vector", [psb["T"], Bw], [Bw],
                       lambda e, r0=r0, r1=r1, c0=c0, wdt=wdt: e.tensor_tensor(
                           bap(Bw, r0, [[128, NCH], [1, wdt]], p0=c0, pn=wdt),
                           bap(Bw, r0, [[128, NCH], [1, wdt]], p0=c0, pn=wdt),
                           bass.AP(ps_t["T"], c0 * 1024 + r0, [[1024, wdt], [128, NCH], [1, wdt]]),
                           ALU.add), "mrg")
            segs.append(st_mrg)

        def st_uw():
            def em_u(e):
                r = None
                for n in range(NCH):
                    bank = "A" if n < 2 else "B"
                    r = e.matmul(ps_t[bank][:, (n % 2) * 256:(n % 2) * 256 + 256],
                                 bsl(Bw, n * 128, (n + 1) * 128),
                                 bsl(vbw, n * 256, (n + 1) * 256), start=True, stop=True)
                return r
            OP("tensor", [Bw, vbw], [psb["A"], psb["B"]], em_u, "uw")
            OP("vector", [psb["A"]], [u_b],
               lambda e, w=min(512, NCH * 256): e.tensor_copy(bsl(u_b, 0, w),
                                                              ps_t["A"][:, 0:w]), "uw")
            if NCH > 2:
                OP("vector", [psb["B"]], [u_b],
                   lambda e: e.tensor_copy(bsl(u_b, 512, 1024), ps_t["B"][:, 0:512]), "uw")
            def em_w(e):
                r = None
                for n in range(NCH):
                    bank = "E" if n < 2 else "B"
                    for half in range(2):
                        cc = (n % 2) * 256 + half * 128
                        r = e.matmul(ps_t[bank][:, cc:cc + 128],
                                     bsl(kbw, n * 256 + half * 128, n * 256 + half * 128 + 128),
                                     bsl(Bw, n * 128, (n + 1) * 128), start=True, stop=True)
                return r
            OP("tensor", [kbw, Bw], [psb["E"], psb["B"]], em_w, "uw")
            OP("vector", [psb["E"]], [what],
               lambda e, w=min(512, NCH * 256): e.tensor_copy(bsl(what, 0, w),
                                                              ps_t["E"][:, 0:w]), "uw")
            if NCH > 2:
                OP("vector", [psb["B"]], [what],
                   lambda e: e.tensor_copy(bsl(what, 512, 1024), ps_t["B"][:, 0:512]), "uw")
        segs.append(st_uw)

    def emit_scan_step(si, hi, bufs, n):
        (q_feat, k_feat, k_tok, v_tok, do_tok, beta_t, dwork,
         Gsb, t1w, t2w, Tbd, Ttf, Tbdt, locT, Aw, Bw, Pw, Ptw,
         u_b, what, kbw, vbw, uadj, rsp, rowT, RSJ, BRSJ, BRSJb) = bufs
        pair = si * H + hi

        def qk_ap(buf, half, n):
            base = (2 * hi + half) * TT + si * TPER + n * 128
            return bsl(buf, base, base + 128)

        def tok_sl(buf, n, w=256):
            base = (si * TCS + n) * 1024 + hi * 256
            return bsl(buf, base, base + w)

        rs_q = lambda n: bsl(dwork, WD + NCH + n, WD + NCH + n + 1)
        Sfp = bsl(S_f, pair * 512, (pair + 1) * 512)

        def em_ws(e, n=n):
            r = None
            for half in range(2):
                r = e.matmul(ps_t["C"][:, 0:256],
                             bsl(what, n * 256 + half * 128, n * 256 + half * 128 + 128),
                             bsl(S_b, pair * 512 + half * 256, pair * 512 + half * 256 + 256),
                             start=(half == 0), stop=(half == 1))
            return r
        OP("tensor", [what, S_b], [psb["C"]], em_ws, "scan")
        OP("vector", [u_b, psb["C"]], [uadj],
           lambda e, n=n: e.tensor_tensor(bsl(uadj, 0, 256),
                                          bsl(u_b, n * 256, (n + 1) * 256),
                                          ps_t["C"][:, 0:256], ALU.subtract), "scan")
        def em_o(e, n=n):
            for half in range(2):
                e.matmul(ps_t["G"][:, 0:256], qk_ap(q_feat, half, n),
                         bsl(S_b, pair * 512 + half * 256, pair * 512 + half * 256 + 256),
                         start=(half == 0), stop=False)
            return e.matmul(ps_t["G"][:, 0:256], bsl(locT, n * 128, (n + 1) * 128),
                            bsl(uadj, 0, 256), start=False, stop=True)
        OP("tensor", [q_feat, S_b, locT, uadj], [psb["G"]], em_o, "scan")
        def em_su(e, n=n):
            r = None
            for half in range(2):
                r = e.matmul(ps_t["F"][:, half * 256:half * 256 + 256],
                             bap(k_tok, (si * TCS + n) * 1024 + hi * 256 + half * 128,
                                 [[1, 128]]),
                             bsl(uadj, 0, 256), start=True, stop=True)
            return r
        OP("tensor", [k_tok, uadj], [psb["F"]], em_su, "scan")
        OP("vector", [psb["F"], S_f], [S_f],
           lambda e: e.tensor_tensor(Sfp, Sfp, ps_t["F"][:, 0:512], ALU.add), "scan")
        OP("vector", [S_f], [S_b],
           lambda e: e.tensor_copy(bsl(S_b, pair * 512, (pair + 1) * 512), Sfp), "scan")
        OP("vector", [psb["G"], dwork], [do_tok],
           lambda e, n=n: e.tensor_scalar(tok_sl(do_tok, n), ps_t["G"][:, 0:256],
                                          rs_q(n), None, ALU.mult), "scan")

    def firw_bc(col):
        return bap(firw, col, [[0, TPER]])

    def emit_fir(fcU, sU, v_pad, fs_feat, fl_feat, veng, ptmp=None):
        base = fcU * 2 * VP + sU * VP
        for (fbuf, taps, woff) in ((fl_feat, 31, 0), (fs_feat, 3, 31)):
            pad = taps - 1
            d0 = fcU * TT + sU * TPER
            if veng == "vector":
                OP(veng, [v_pad, firw], [fbuf],
                   lambda e, base=base, d0=d0, woff=woff, fcU=fcU, pad=pad, fbuf=fbuf:
                   e.tensor_scalar(bsl(fbuf, d0, d0 + TPER),
                                   bsl(v_pad, base + 30 - pad, base + 30 - pad + TPER),
                                   bsl(firw, fcU * 34 + woff, fcU * 34 + woff + 1),
                                   None, ALU.mult), "fir")
                for j in range(1, taps):
                    OP(veng, [v_pad, firw, fbuf], [fbuf],
                       lambda e, base=base, d0=d0, woff=woff, j=j, fcU=fcU, pad=pad, fbuf=fbuf:
                       e.scalar_tensor_tensor(
                           bsl(fbuf, d0, d0 + TPER),
                           bsl(v_pad, base + 30 - pad + j, base + 30 - pad + j + TPER),
                           bsl(firw, fcU * 34 + woff + j, fcU * 34 + woff + j + 1),
                           bsl(fbuf, d0, d0 + TPER), ALU.mult, ALU.add), "fir")
            else:
                OP(veng, [v_pad, firw], [fbuf],
                   lambda e, base=base, d0=d0, woff=woff, fcU=fcU, pad=pad, fbuf=fbuf:
                   e.tensor_tensor(bsl(fbuf, d0, d0 + TPER),
                                   bsl(v_pad, base + 30 - pad, base + 30 - pad + TPER),
                                   firw_bc(fcU * 34 + woff), ALU.mult), "fir")
                for j in range(1, taps):
                    OP(veng, [v_pad, firw], [ptmp],
                       lambda e, base=base, woff=woff, j=j, fcU=fcU, pad=pad, pt=ptmp:
                       e.tensor_tensor(bsl(pt, 0, TPER),
                                       bsl(v_pad, base + 30 - pad + j, base + 30 - pad + j + TPER),
                                       firw_bc(fcU * 34 + woff + j), ALU.mult), "fir")
                    OP(veng, [ptmp, fbuf], [fbuf],
                       lambda e, d0=d0, fbuf=fbuf, pt=ptmp:
                       e.tensor_tensor(bsl(fbuf, d0, d0 + TPER), bsl(fbuf, d0, d0 + TPER),
                                       bsl(pt, 0, TPER), ALU.add), "fir")

    def emit_tile(ti):
        t0 = ti * TPER
        for si in range(2):
            DMA("scalar", bap(x_tok, si * TCS * 1024, [[1024, TCS], [1, 1024]]),
                bass.AP(x_in.tensor, (si * L + t0) * D, [[D, 128], [128 * D, TCS], [1, D]]),
                [dbuf(x_in)], [x_tok], "xload")

        stats = FA.alloc("stats", TCH * 24)
        h_feat = BA.alloc("h_feat", 8 * TT)
        emit_h(h_feat, stats)
        pump(16)

        beta_t = FA.alloc("beta", TCH * 4)
        for tg in range(NTG):
            def em_beta(e, tg=tg):
                r = None
                for k in range(8):
                    r = e.matmul(ps_t["E"][0:4, 0:TGW],
                                 bap(wbb, k * 4, [[1, 4]]),
                                 bap(h_feat, k * TT + tg * TGW, [[1, TGW]]),
                                 start=(k == 0), stop=(k == 7))
                return r
            OP("tensor", [wbb, h_feat], [psb["E"]], em_beta, "beta")
            bfe = BA.alloc("bfe", TGW)
            OP("scalar", [psb["E"]], [bfe],
               lambda e, b=bfe: e.activation(bsl(b, 0, TGW, 0, 4), ps_t["E"][0:4, 0:TGW],
                                             AF.Identity, bias=0.0, scale=1.0), "beta")
            for c4 in range(TGW // 128):
                tc = tg * (TGW // 128) + c4
                OP("tensor", [bfe, cstb["ident"]], [psb["T"]],
                   lambda e, b=bfe, c4=c4, tc=tc: e.transpose(
                       ps_t["T"][:, tc * 4:(tc + 1) * 4],
                       bsl(b, c4 * 128, (c4 + 1) * 128, 0, 4),
                       cap("ident", 0, 4, 0, 4)), "beta")
            BA.free(bfe)
        OP("scalar", [psb["T"]], [beta_t],
           lambda e: e.activation(bsl(beta_t, 0, TCH * 4), ps_t["T"][:, 0:TCH * 4],
                                  AF.Sigmoid, bias=0.0, scale=1.0), "beta")
        pump(8)

        q_feat = BA.alloc("q_feat", 8 * TT)
        k_feat = BA.alloc("k_feat", 8 * TT)
        v_pad = BA.alloc("v_pad", 8 * 2 * VP)
        for fc in range(8):
            for si in range(2):
                OP("vector", [vtails], [v_pad],
                   lambda e, fc=fc, si=si: e.tensor_copy(
                       bsl(v_pad, fc * 2 * VP + si * VP, fc * 2 * VP + si * VP + 30),
                       bsl(vtails, (fc * 2 + si) * 30, (fc * 2 + si) * 30 + 30)), "vtl")
        wbufs = [BA.alloc("wbuf0", 8192)]
        qcb = BA.alloc("qc", 2 * QP)
        cdgb = [BA.alloc("cdg0", 4 * 128), BA.alloc("cdg1", 4 * 128)]
        for pi, dkey in enumerate(("q", "k", "v")):
            dst = {"q": q_feat, "k": k_feat, "v": v_pad}[dkey]
            wbuf = wbufs[0]
            DMA("sync", bap(wbuf, 0, [[1024, 8], [1, 1024]]),
                bass.AP(s_w[dkey].tensor, 0, [[D, 128], [128 * D, 8], [1, D]]),
                [dbuf(s_w[dkey])], [wbuf], "wqkv")
            for m in range(8):
                for tg in range(NTG):
                    bank = rotAD[(m * NTG + tg) % 4]
                    def em(e, m=m, tg=tg, bank=bank, wbuf=wbuf):
                        r = None
                        for k in range(8):
                            r = e.matmul(ps_t[bank][:, 0:TGW],
                                         bsl(wbuf, k * 1024 + m * 128, k * 1024 + (m + 1) * 128),
                                         bap(h_feat, k * TT + tg * TGW, [[1, TGW]]),
                                         start=(k == 0), stop=(k == 7))
                        return r
                    OP("tensor", [wbuf, h_feat], [psb[bank]], em, "proj")
                    for s2 in range(max(1, TGW // TPER)):
                        si = tg * max(1, TGW // TPER) + s2
                        OP("vector", [psb[bank]], [qcb],
                           lambda e, bank=bank, si=si, s2=s2: e.tensor_copy(
                               bsl(qcb, si * QP + 6, si * QP + 6 + TPER),
                               ps_t[bank][:, s2 * TPER:(s2 + 1) * TPER]), "proj")
                for si in range(2):
                    tix = ((pi * 8 + m) * 2 + si) * 4
                    OP("vector", [qtails], [qcb],
                       lambda e, si=si, tix=tix: e.tensor_copy(
                           bsl(qcb, si * QP + 3, si * QP + 6),
                           bsl(qtails, tix, tix + 3)), "conv")
                    OP("vector", [qcb], [qtails],
                       lambda e, si=si, tix=tix: e.tensor_copy(
                           bsl(qtails, tix, tix + 3),
                           bsl(qcb, si * QP + 3 + TPER, si * QP + 6 + TPER)), "conv")
                cdg = cdgb[m % 2]
                for j in range(4):
                    OP("vector", [cstb["ident"], convw], [cdg],
                       lambda e, m=m, pi=pi, j=j, cdg=cdg: e.tensor_scalar(
                           bsl(cdg, j * 128, (j + 1) * 128), cap("ident"),
                           bsl(convw, pi * 32 + m * 4 + j, pi * 32 + m * 4 + j + 1),
                           None, ALU.mult), "conv")
                for si in range(2):
                    cb2 = rotAD[(m * 2 + si) % 4]
                    def em_cv(e, si=si, cdg=cdg, cb2=cb2):
                        r = None
                        for j in range(4):
                            r = e.matmul(ps_t[cb2][:, 0:TPER],
                                         bsl(cdg, j * 128, (j + 1) * 128),
                                         bsl(qcb, si * QP + 3 + j, si * QP + 3 + j + TPER),
                                         start=(j == 0), stop=(j == 3))
                        return r
                    OP("tensor", [cdg, qcb], [psb[cb2]], em_cv, "conv")
                pump(8)
                if pi < 2:
                    for si in range(2):
                        cb2 = rotAD[(m * 2 + si) % 4]
                        OP("scalar", [psb[cb2]], [dst],
                           lambda e, d=dst, m=m, si=si, cb2=cb2: e.activation(
                               bsl(d, m * TT + si * TPER, m * TT + (si + 1) * TPER),
                               ps_t[cb2][:, 0:TPER],
                               AF.Silu, bias=0.0, scale=1.0), "silu")
                else:
                    for si in range(2):
                        cb2 = rotAD[(m * 2 + si) % 4]
                        OP("scalar", [psb[cb2]], [dst],
                           lambda e, d=dst, m=m, si=si, cb2=cb2: e.activation(
                               bsl(d, m * 2 * VP + si * VP + 30, m * 2 * VP + si * VP + 30 + TPER),
                               ps_t[cb2][:, 0:TPER],
                               AF.Silu, bias=0.0, scale=1.0), "silu")
        pump_flush()
        BA.free(wbufs[0]); BA.free(qcb); BA.free(cdgb[0]); BA.free(cdgb[1])
        BA.free(h_feat)

        k_tok = BA.alloc("k_tok", TCH * 1024)
        v_tok = BA.alloc("v_tok", TCH * 1024)
        for fc in range(8):
            for si in range(2):
                DMA("sync", bap(k_tok, si * TCS * 1024 + fc * 128, [[1024, TCS], [1, 128]]),
                    bsl(k_feat, fc * TT + si * TPER, fc * TT + (si + 1) * TPER),
                    [k_feat], [k_tok], "kT", transpose=True)
                DMA("sync", bap(v_tok, si * TCS * 1024 + fc * 128, [[1024, TCS], [1, 128]]),
                    bsl(v_pad, fc * 2 * VP + si * VP + 30, fc * 2 * VP + si * VP + 30 + TPER),
                    [v_pad], [v_tok], "vT", transpose=True)

        do_tok = BA.alloc("do_tok", TCH * 1024)
        dbufs = dict(
            Gsb=BA.alloc("Gsb", WD), t1w=BA.alloc("t1w", WD), t2w=BA.alloc("t2w", WD),
            Tbd=BA.alloc("Tbd", WD), Ttf=BA.alloc("Ttf", WD), Tbdt=BA.alloc("Tbdt", WD),
            Aw=BA.alloc("Aw", WD), Bw=BA.alloc("Bw", WD),
            kbw=BA.alloc("kbw", NCH * 256), vbw=BA.alloc("vbw", NCH * 256),
            uadj=BA.alloc("uadj", 256), rsp=BA.alloc("rsp", 2 * NCH),
            rowT=BA.alloc("rowT", 128), RSJ=BA.alloc("RSJ", WD),
            BRSJ=BA.alloc("BRSJ", WD))
        slots = [dict(dwork=FA.alloc(f"dwork{i}", WD + 7 * NCH),
                      locT=BA.alloc(f"locT{i}", WD),
                      u_b=BA.alloc(f"u_b{i}", NCH * 256),
                      what=BA.alloc(f"what{i}", NCH * 256)) for i in range(2)]

        def pair_bufs(p):
            sl = slots[p % 2]
            return (q_feat, k_feat, k_tok, v_tok, do_tok, beta_t, sl["dwork"],
                    dbufs["Gsb"], dbufs["t1w"], dbufs["t2w"], dbufs["Tbd"],
                    dbufs["Ttf"], dbufs["Tbdt"], sl["locT"], dbufs["Aw"],
                    dbufs["Bw"], dbufs["RSJ"], dbufs["BRSJ"], sl["u_b"],
                    sl["what"], dbufs["kbw"], dbufs["vbw"], dbufs["uadj"],
                    dbufs["rsp"], dbufs["rowT"], dbufs["RSJ"], dbufs["BRSJ"],
                    dbufs["BRSJ"])

        segs0 = []
        make_prep_segments(0, 0, pair_bufs(0), segs0)
        for fn in segs0:
            fn()
        for p in range(8):
            si, hi = divmod(p, H)
            nsegs = []
            if p + 1 < 8:
                make_prep_segments((p + 1) // H, (p + 1) % H, pair_bufs(p + 1), nsegs)
            per = -(-len(nsegs) // NCH) if nsegs else 0
            k = 0
            for n in range(NCH):
                emit_scan_step(si, hi, pair_bufs(p), n)
                for _ in range(per):
                    if k < len(nsegs):
                        nsegs[k]()
                        k += 1
            while k < len(nsegs):
                nsegs[k]()
                k += 1
        for b in dbufs.values():
            BA.free(b)
        for sl in slots:
            FA.free(sl["dwork"])
            BA.free(sl["locT"]); BA.free(sl["u_b"]); BA.free(sl["what"])
        BA.free(q_feat)
        BA.free(k_tok)
        BA.free(k_feat)

        fs_feat = BA.alloc("fs_feat", 8 * TT)
        fl_feat = BA.alloc("fl_feat", 8 * TT)
        dgb = [BA.alloc("dg0", 34 * 128), BA.alloc("dg1", 34 * 128)]
        for fc in range(6):
            dg = dgb[fc % 2]
            for j in range(34):
                OP("vector", [cstb["ident"], firw], [dg],
                   lambda e, fc=fc, j=j, dg=dg: e.tensor_scalar(
                       bsl(dg, j * 128, (j + 1) * 128), cap("ident"),
                       bsl(firw, fc * 34 + j, fc * 34 + j + 1), None, ALU.mult), "fird")
            for si in range(2):
                base = fc * 2 * VP + si * VP
                bl = rotAD[(fc * 2 + si) % 2 * 2]
                bs = rotAD[(fc * 2 + si) % 2 * 2 + 1]
                def em_firl(e, base=base, dg=dg, bl=bl):
                    r = None
                    for j in range(31):
                        r = e.matmul(ps_t[bl][:, 0:TPER],
                                     bsl(dg, j * 128, (j + 1) * 128),
                                     bsl(v_pad, base + j, base + j + TPER),
                                     start=(j == 0), stop=(j == 30))
                    return r
                OP("tensor", [dg, v_pad], [psb[bl]], em_firl, "fird")
                def em_firs(e, base=base, dg=dg, bs=bs):
                    r = None
                    for j in range(3):
                        r = e.matmul(ps_t[bs][:, 0:TPER],
                                     bsl(dg, (31 + j) * 128, (31 + j + 1) * 128),
                                     bsl(v_pad, base + 28 + j, base + 28 + j + TPER),
                                     start=(j == 0), stop=(j == 2))
                    return r
                OP("tensor", [dg, v_pad], [psb[bs]], em_firs, "fird")
                d0 = fc * TT + si * TPER
                OP("vector", [psb[bl]], [fl_feat],
                   lambda e, d0=d0, bl=bl: e.tensor_copy(
                       bsl(fl_feat, d0, d0 + TPER), ps_t[bl][:, 0:TPER]), "fird")
                OP("vector", [psb[bs]], [fs_feat],
                   lambda e, d0=d0, bs=bs: e.tensor_copy(
                       bsl(fs_feat, d0, d0 + TPER), ps_t[bs][:, 0:TPER]), "fird")
        BA.free(dgb[0]); BA.free(dgb[1])
        for uidx in range(12, 16):
            emit_fir(uidx // 2, uidx % 2, v_pad, fs_feat, fl_feat, "vector", None)
        for fc in range(8):
            for si in range(2):
                OP("vector", [v_pad], [vtails],
                   lambda e, fc=fc, si=si: e.tensor_copy(
                       bsl(vtails, (fc * 2 + si) * 30, (fc * 2 + si) * 30 + 30),
                       bsl(v_pad, fc * 2 * VP + si * VP + TPER, fc * 2 * VP + si * VP + TPER + 30)), "vtl")
        BA.free(v_pad)

        do_feat = BA.alloc("do_feat", 8 * TT)
        for tc in range(TCH):
            DMA("sync", bap(do_feat, tc * 128, [[TT, 8], [1, 128]]),
                bsl(do_tok, tc * 1024, (tc + 1) * 1024), [do_tok], [do_feat], "doT",
                transpose=True)

        h_feat2 = BA.alloc("h_feat2", 8 * TT)
        emit_h(h_feat2, stats, apply_only=True)

        w1ms = [BA.alloc("w1m0", 4096), BA.alloc("w1m1", 4096)]
        hidb = BA.alloc("hid", 2 * 1024)
        logf = BA.alloc("logf", TT)
        gmap = [(h_feat2, 0), (fs_feat, 8), (fl_feat, 16), (do_feat, 24)]
        def gate_ap(k, tg):
            for buf, k0 in reversed(gmap):
                if k >= k0:
                    return bap(buf, (k - k0) * TT + tg * TGW, [[1, TGW]])
        for m in range(16):
            w1m = w1ms[m % 2]
            DMA("sync", bap(w1m, 0, [[128, 32], [1, 128]]),
                bass.AP(s_w["f1"].tensor, m * 128, [[DHID, 128], [128 * DHID, 32], [1, 128]]),
                [dbuf(s_w["f1"])], [w1m], "w1")
            for tg in range(NTG):
                bank = rotAD[(m * NTG + tg) % 4]
                def em_f1(e, m=m, tg=tg, bank=bank, w1m=w1m):
                    r = None
                    for k in range(32):
                        r = e.matmul(ps_t[bank][:, 0:TGW],
                                     bsl(w1m, k * 128, (k + 1) * 128),
                                     gate_ap(k, tg), start=(k == 0), stop=(k == 31))
                    return r
                OP("tensor", [w1m, h_feat2, fs_feat, fl_feat, do_feat],
                   [psb[bank]], em_f1, "f1")
                OP("scalar", [psb[bank], b1b], [hidb],
                   lambda e, m=m, tg=tg, bank=bank: e.activation(
                       bsl(hidb, (m % 2) * 1024 + tg * TGW, (m % 2) * 1024 + tg * TGW + TGW),
                       ps_t[bank][:, 0:TGW], AF.Gelu,
                       bias=bsl(b1b, m, m + 1), scale=1.0), "f1")
            for tg in range(NTG):
                bank2 = "E" if tg == 0 else "G"
                OP("tensor", [hidb, w2b], [psb[bank2]],
                   lambda e, m=m, tg=tg, bank2=bank2: e.matmul(
                       ps_t[bank2][0:16, 0:TGW],
                       bap(w2b, m * 16, [[1, 16]]),
                       bsl(hidb, (m % 2) * 1024 + tg * TGW, (m % 2) * 1024 + tg * TGW + TGW),
                       start=(m == 0), stop=(m == 15)), "f2")
        for tg in range(NTG):
            bank2 = "E" if tg == 0 else "G"
            OP("scalar", [psb[bank2], b2b], [logf],
               lambda e, tg=tg, bank2=bank2: e.activation(
                   bsl(logf, tg * TGW, (tg + 1) * TGW, 0, 16),
                   ps_t[bank2][0:16, 0:TGW], AF.Identity,
                   bias=bsl(b2b, 0, 1, 0, 16), scale=1.0), "f2")
        BA.free(w1ms[0]); BA.free(w1ms[1]); BA.free(hidb); BA.free(h_feat2); BA.free(do_feat)

        fs_tok = BA.alloc("fs_tok", TCH * 1024)
        fl_tok = BA.alloc("fl_tok", TCH * 1024)
        for fc in range(8):
            for si in range(2):
                DMA("sync", bap(fs_tok, si * TCS * 1024 + fc * 128, [[1024, TCS], [1, 128]]),
                    bsl(fs_feat, fc * TT + si * TPER, fc * TT + (si + 1) * TPER),
                    [fs_feat], [fs_tok], "fsT", transpose=True)
                DMA("sync", bap(fl_tok, si * TCS * 1024 + fc * 128, [[1024, TCS], [1, 128]]),
                    bsl(fl_feat, fc * TT + si * TPER, fc * TT + (si + 1) * TPER),
                    [fl_feat], [fl_tok], "flT", transpose=True)
        BA.free(fs_feat); BA.free(fl_feat)

        wob = BA.alloc("wo", 8192)
        DMA("sync", bap(wob, 0, [[1024, 8], [1, 1024]]),
            bass.AP(s_w["o"].tensor, 0, [[D, 128], [128 * D, 8], [1, D]]),
            [dbuf(s_w["o"])], [wob], "wo")
        gwork = FA.alloc("gwork", 64)
        otok = BA.alloc("otok", 1024)
        orms = BA.alloc("orms", 2 * 1024)
        ofeatc = BA.alloc("ofeatc", 2 * 1024)
        sqw = FA.alloc("sq", 1024)
        for tc in range(TCH):
            OP("tensor", [logf, cstb["ident"]], [psb["T"]],
               lambda e, tc=tc: e.transpose(ps_t["T"][:, 0:16],
                                            bsl(logf, tc * 128, (tc + 1) * 128, 0, 16),
                                            cap("ident", 0, 16, 0, 16)), "gate")
            OP("vector", [psb["T"]], [gwork],
               lambda e: e.tensor_copy(bsl(gwork, 0, 16), ps_t["T"][:, 0:16]), "gate")
            OP("scalar", [gwork], [gwork],
               lambda e: e.activation(bsl(gwork, 16, 20), bap(gwork, 3, [[4, 4]]),
                                      AF.Sigmoid, bias=0.0, scale=1.0), "gate")
            OP("vector", [gwork], [gwork],
               lambda e: e.tensor_scalar(bsl(gwork, 20, 24), bsl(gwork, 16, 20),
                                         0.09, 0.9, ALU.mult, ALU.add), "gate")
            OP("vector", [gwork], [gwork],
               lambda e: e.tensor_tensor(bsl(gwork, 20, 24), bsl(gwork, 20, 24),
                                         bsl(gwork, 16, 20), ALU.mult), "gate")
            OP("vector", [gwork, rtau], [gwork],
               lambda e: e.tensor_tensor(bsl(gwork, 24, 40), bsl(gwork, 0, 16),
                                         bsl(rtau, 0, 16), ALU.mult), "gate")
            OP("vector", [gwork], [gwork],
               lambda e: e.tensor_tensor(bsl(gwork, 40, 44), bap(gwork, 24, [[4, 4]]),
                                         bap(gwork, 25, [[4, 4]]), ALU.max), "gate")
            OP("vector", [gwork], [gwork],
               lambda e: e.tensor_tensor(bsl(gwork, 40, 44), bsl(gwork, 40, 44),
                                         bap(gwork, 26, [[4, 4]]), ALU.max), "gate")
            OP("vector", [gwork], [gwork],
               lambda e: e.tensor_tensor(bap(gwork, 44, [[3, 4], [1, 3]]),
                                         bap(gwork, 24, [[4, 4], [1, 3]]),
                                         bap(gwork, 40, [[1, 4], [0, 3]]), ALU.subtract), "gate")
            OP("scalar", [gwork], [gwork],
               lambda e: e.activation(bsl(gwork, 44, 56), bsl(gwork, 44, 56),
                                      AF.Exp, bias=0.0, scale=1.0), "gate")
            OP("vector", [gwork], [gwork],
               lambda e: e.tensor_tensor(bsl(gwork, 56, 60), bap(gwork, 44, [[3, 4]]),
                                         bap(gwork, 45, [[3, 4]]), ALU.add), "gate")
            OP("vector", [gwork], [gwork],
               lambda e: e.tensor_tensor(bsl(gwork, 56, 60), bsl(gwork, 56, 60),
                                         bap(gwork, 46, [[3, 4]]), ALU.add), "gate")
            OP("vector", [gwork], [gwork],
               lambda e: e.reciprocal(bsl(gwork, 60, 64), bsl(gwork, 56, 60)), "gate")
            OP("vector", [gwork], [gwork],
               lambda e: e.tensor_scalar(bsl(gwork, 40, 44), bsl(gwork, 20, 24),
                                         -1.0, 1.0, ALU.mult, ALU.add), "gate")
            OP("vector", [gwork], [gwork],
               lambda e: e.tensor_tensor(bsl(gwork, 56, 60), bsl(gwork, 60, 64),
                                         bsl(gwork, 40, 44), ALU.mult), "gate")
            OP("vector", [gwork], [gwork],
               lambda e: e.tensor_tensor(bap(gwork, 44, [[3, 4], [1, 3]]),
                                         bap(gwork, 44, [[3, 4], [1, 3]]),
                                         bap(gwork, 56, [[1, 4], [0, 3]]), ALU.mult), "gate")
            for hh in range(H):
                OP("vector", [fs_tok, gwork], [otok],
                   lambda e, tc=tc, hh=hh: e.tensor_scalar(
                       bsl(otok, hh * 256, (hh + 1) * 256),
                       bsl(fs_tok, tc * 1024 + hh * 256, tc * 1024 + (hh + 1) * 256),
                       bsl(gwork, 44 + hh * 3, 45 + hh * 3), None, ALU.mult), "cmb")
                for (srcb, col) in ((fl_tok, 45 + hh * 3), (do_tok, 46 + hh * 3),
                                    (v_tok, 20 + hh)):
                    OP("vector", [srcb, gwork, otok], [otok],
                       lambda e, tc=tc, hh=hh, srcb=srcb, col=col: e.scalar_tensor_tensor(
                           bsl(otok, hh * 256, (hh + 1) * 256),
                           bsl(srcb, tc * 1024 + hh * 256, tc * 1024 + (hh + 1) * 256),
                           bsl(gwork, col, col + 1),
                           bsl(otok, hh * 256, (hh + 1) * 256), ALU.mult, ALU.add), "cmb")
            OP("vector", [otok], [sqw],
               lambda e: e.tensor_tensor(bsl(sqw, 0, 1024), bsl(otok, 0, 1024),
                                         bsl(otok, 0, 1024), ALU.mult), "rms")
            OP("vector", [sqw], [gwork],
               lambda e: e.reduce_sum(bap(gwork, 56, [[1, 4], [1, 1]]),
                                      bap(sqw, 0, [[256, 4], [1, 256]]), AXX), "rms")
            OP("scalar", [gwork], [gwork],
               lambda e: e.activation(bsl(gwork, 60, 64), bsl(gwork, 56, 60),
                                      AF.Sqrt, bias=bsl(epsb, 0, 1), scale=1.0 / 256.0), "rms")
            OP("vector", [gwork], [gwork],
               lambda e: e.reciprocal(bsl(gwork, 56, 60), bsl(gwork, 60, 64)), "rms")
            ob = (tc % 2) * 1024
            for hh in range(H):
                OP("vector", [otok, gwork, nwrep], [orms],
                   lambda e, hh=hh, ob=ob: e.scalar_tensor_tensor(
                       bsl(orms, ob + hh * 256, ob + (hh + 1) * 256),
                       bsl(otok, hh * 256, (hh + 1) * 256),
                       bsl(gwork, 56 + hh, 57 + hh), bsl(nwrep, 0, 256),
                       ALU.mult, ALU.mult), "rms")
            DMA("sync", bap(ofeatc, ob, [[128, 8], [1, 128]]),
                bsl(orms, ob, ob + 1024), [orms], [ofeatc], "oT", transpose=True)
            for fg in range(2):
                bank = rotAD[(tc * 2 + fg) % 4]
                def em_wo(e, tc=tc, fg=fg, bank=bank, ob=ob):
                    r = None
                    for k in range(8):
                        r = e.matmul(ps_t[bank][:, 0:512],
                                     bsl(ofeatc, ob + k * 128, ob + (k + 1) * 128),
                                     bsl(wob, k * 1024 + fg * 512, k * 1024 + (fg + 1) * 512),
                                     start=(k == 0), stop=(k == 7))
                    return r
                OP("tensor", [ofeatc, wob], [psb[bank]], em_wo, "wo")
                OP("vector", [psb[bank], x_tok], [x_tok],
                   lambda e, tc=tc, fg=fg, bank=bank: e.tensor_tensor(
                       bsl(x_tok, tc * 1024 + fg * 512, tc * 1024 + (fg + 1) * 512),
                       bsl(x_tok, tc * 1024 + fg * 512, tc * 1024 + (fg + 1) * 512),
                       ps_t[bank][:, 0:512], ALU.add), "wo")
        BA.free(wob); BA.free(otok); BA.free(orms); BA.free(ofeatc)
        FA.free(gwork); FA.free(sqw)
        BA.free(fs_tok); BA.free(fl_tok); BA.free(do_tok); BA.free(v_tok)
        BA.free(logf)
        FA.free(beta_t)

        GK = min(4, TCH)
        yq_c = BA.alloc("yq", 1024)
        t5f = FA.alloc("t5f", 1024)
        mbf = FA.alloc("mbf", 1024)
        c1ff = FA.alloc("c1ff", 1024)
        ffst = FA.alloc("ffst", TCH * 24)
        def emit_ffn_group(g0):
            tcs = list(range(g0, g0 + GK))
            yqf = BA.alloc("yqf", 8 * GK * 128)
            for tc in tcs:
                so = tc * 24
                ln_chunk(bsl(x_tok, tc * 1024, (tc + 1) * 1024),
                         bsl(x_tok, tc * 1024, tc * 1024 + 512),
                         bsl(x_tok, tc * 1024 + 512, (tc + 1) * 1024),
                         bsl(yq_c, 0, 1024), ffst, so, yq_c)
                OP("vector", [yq_c], [ffst],
                   lambda e, so=so: e.reduce_max(bsl(ffst, so + 17, so + 18),
                                                 bsl(yq_c, 0, 1024), AXX,
                                                 apply_absolute_value=True), "aq")
                OP("vector", [ffst], [ffst],
                   lambda e, so=so: e.tensor_scalar(bsl(ffst, so + 17, so + 18),
                                                    bsl(ffst, so + 17, so + 18),
                                                    1e-5, None, ALU.max), "aq")
                OP("vector", [ffst], [ffst],
                   lambda e, so=so: e.reciprocal(bsl(ffst, so + 18, so + 19),
                                                 bsl(ffst, so + 17, so + 18)), "aq")
                OP("vector", [ffst], [ffst],
                   lambda e, so=so: e.tensor_scalar(bsl(ffst, so + 18, so + 19),
                                                    bsl(ffst, so + 18, so + 19),
                                                    127.0, None, ALU.mult), "aq")
                OP("vector", [ffst, ffsc], [ffst],
                   lambda e, so=so: e.tensor_tensor(bsl(ffst, so + 19, so + 20),
                                                    bsl(ffst, so + 17, so + 18),
                                                    bsl(ffsc, 1, 2), ALU.mult), "aq")
                OP("vector", [ffst, ffsc], [ffst],
                   lambda e, so=so: e.tensor_tensor(bsl(ffst, so + 20, so + 21),
                                                    bsl(ffst, so + 17, so + 18),
                                                    bsl(ffsc, 3, 4), ALU.mult), "aq")
                OP("scalar", [yq_c, ffst], [t5f],
                   lambda e, so=so: e.activation(bsl(t5f, 0, 1024), bsl(yq_c, 0, 1024),
                                                 AF.Identity, bias=bsl(epsb, 2, 3),
                                                 scale=bsl(ffst, so + 18, so + 19)), "aq")
                emit_floor(t5f, bsl(t5f, 0, 1024), yq_c, bsl(yq_c, 0, 1024),
                           c1ff, bsl(c1ff, 0, 1024), mbf, bsl(mbf, 0, 1024), 1024)
                DMA("sync", bap(yqf, (tc - g0) * 128, [[GK * 128, 8], [1, 128]]),
                    bsl(yq_c, 0, 1024), [yq_c], [yqf], "yqT", transpose=True)
            z_tok = BA.alloc("z", GK * 4096)
            tslc = BA.alloc("tsl", GK * 512)
            wgus = [BA.alloc("wgu0", 2 * 4096), BA.alloc("wgu1", 2 * 4096)]
            for fgi in range(8):
                wgu = wgus[fgi % 2]
                for wi, wkey in enumerate(("g", "u")):
                    DMA("sync", bap(wgu, wi * 4096, [[512, 8], [1, 512]]),
                        bass.AP(s_w[wkey].tensor, fgi * 512,
                                [[INNER, 128], [128 * INNER, 8], [1, 512]]),
                        [dbuf(s_w[wkey])], [wgu], "wgu")
                for ti2, tc in enumerate(tcs):
                    bank = rotAD[ti2 % 4]
                    def em_g2(e, ti2=ti2, bank=bank, wgu=wgu):
                        r = None
                        for k in range(8):
                            r = e.matmul(ps_t[bank][:, 0:512],
                                         bsl(yqf, k * GK * 128 + ti2 * 128,
                                             k * GK * 128 + (ti2 + 1) * 128),
                                         bsl(wgu, k * 512, (k + 1) * 512),
                                         start=(k == 0), stop=(k == 7))
                        return r
                    OP("tensor", [yqf, wgu], [psb[bank]], em_g2, "ffg")
                    OP("scalar", [psb[bank], ffst], [tslc],
                       lambda e, ti2=ti2, tc=tc, bank=bank: e.activation(
                           bsl(tslc, ti2 * 512, (ti2 + 1) * 512), ps_t[bank][:, 0:512],
                           AF.Silu, bias=0.0,
                           scale=bsl(ffst, tc * 24 + 19, tc * 24 + 20)), "ffg")
                for ti2, tc in enumerate(tcs):
                    bank = rotAD[ti2 % 4]
                    def em_u2(e, ti2=ti2, bank=bank, wgu=wgu):
                        r = None
                        for k in range(8):
                            r = e.matmul(ps_t[bank][:, 0:512],
                                         bsl(yqf, k * GK * 128 + ti2 * 128,
                                             k * GK * 128 + (ti2 + 1) * 128),
                                         bsl(wgu, 4096 + k * 512, 4096 + (k + 1) * 512),
                                         start=(k == 0), stop=(k == 7))
                        return r
                    OP("tensor", [yqf, wgu], [psb[bank]], em_u2, "ffu")
                    OP("vector", [psb[bank], ffst, tslc], [z_tok],
                       lambda e, ti2=ti2, tc=tc, fgi=fgi, bank=bank: e.scalar_tensor_tensor(
                           bsl(z_tok, ti2 * 4096 + fgi * 512, ti2 * 4096 + (fgi + 1) * 512),
                           ps_t[bank][:, 0:512],
                           bsl(ffst, tc * 24 + 20, tc * 24 + 21),
                           bsl(tslc, ti2 * 512, (ti2 + 1) * 512), ALU.mult, ALU.mult), "ffu")
            BA.free(wgus[0]); BA.free(wgus[1]); BA.free(tslc); BA.free(yqf)
            zqf = BA.alloc("zqf", GK * 4096)
            wdbs = [BA.alloc("wd0", 512), BA.alloc("wd1", 512)]

            def emit_zq(ti2, tc):
                so = tc * 24
                OP("vector", [z_tok], [ffst],
                   lambda e, ti2=ti2, so=so: e.reduce_max(
                       bsl(ffst, so + 17, so + 18),
                       bsl(z_tok, ti2 * 4096, (ti2 + 1) * 4096), AXX,
                       apply_absolute_value=True), "zq")
                OP("vector", [ffst], [ffst],
                   lambda e, so=so: e.tensor_scalar(bsl(ffst, so + 17, so + 18),
                                                    bsl(ffst, so + 17, so + 18),
                                                    1e-5, None, ALU.max), "zq")
                OP("vector", [ffst], [ffst],
                   lambda e, so=so: e.reciprocal(bsl(ffst, so + 18, so + 19),
                                                 bsl(ffst, so + 17, so + 18)), "zq")
                OP("vector", [ffst], [ffst],
                   lambda e, so=so: e.tensor_scalar(bsl(ffst, so + 18, so + 19),
                                                    bsl(ffst, so + 18, so + 19),
                                                    127.0, None, ALU.mult), "zq")
                OP("vector", [ffst, ffsc], [ffst],
                   lambda e, so=so: e.tensor_tensor(bsl(ffst, so + 19, so + 20),
                                                    bsl(ffst, so + 17, so + 18),
                                                    bsl(ffsc, 5, 6), ALU.mult), "zq")
                for c0 in range(0, 4096, 1024):
                    OP("scalar", [z_tok, ffst, epsb], [t5f],
                       lambda e, ti2=ti2, c0=c0, so=so: e.activation(
                           bsl(t5f, 0, 1024),
                           bsl(z_tok, ti2 * 4096 + c0, ti2 * 4096 + c0 + 1024),
                           AF.Identity, bias=bsl(epsb, 2, 3),
                           scale=bsl(ffst, so + 18, so + 19)), "zq")
                    emit_floor(t5f, bsl(t5f, 0, 1024), z_tok,
                               bsl(z_tok, ti2 * 4096 + c0, ti2 * 4096 + c0 + 1024),
                               c1ff, bsl(c1ff, 0, 1024), mbf, bsl(mbf, 0, 1024), 1024)
                DMA("sync", bap(zqf, ti2 * 128, [[GK * 128, 32], [1, 128]]),
                    bsl(z_tok, ti2 * 4096, (ti2 + 1) * 4096), [z_tok], [zqf], "zqT",
                    transpose=True)

            def emit_down_pair(p0):
                tlo, thi = p0, p0 + 1
                for fg in range(2):
                    bklo = rotAD[fg * 2]
                    bkhi = rotAD[fg * 2 + 1]
                    for k in range(32):
                        wdb = wdbs[k % 2]
                        DMA("sync", bsl(wdb, 0, 512),
                            bass.AP(s_w["d"].tensor, k * 128 * D + fg * 512,
                                    [[D, 128], [1, 512]]),
                            [dbuf(s_w["d"])], [wdb], "wd")
                        def em_d(e, k=k, wdb=wdb, bklo=bklo, bkhi=bkhi, tlo=tlo, thi=thi):
                            e.matmul(ps_t[bklo][:, 0:512],
                                     bsl(zqf, k * GK * 128 + tlo * 128,
                                         k * GK * 128 + (tlo + 1) * 128),
                                     bsl(wdb, 0, 512), start=(k == 0), stop=(k == 31))
                            return e.matmul(ps_t[bkhi][:, 0:512],
                                            bsl(zqf, k * GK * 128 + thi * 128,
                                                k * GK * 128 + (thi + 1) * 128),
                                            bsl(wdb, 0, 512), start=(k == 0), stop=(k == 31))
                        OP("tensor", [zqf, wdb], [psb[bklo], psb[bkhi]], em_d, "ffd")
                    for off, bk in ((0, bklo), (1, bkhi)):
                        tc = g0 + p0 + off
                        OP("vector", [psb[bk], ffst, x_tok], [x_tok],
                           lambda e, tc=tc, fg=fg, bk=bk: e.scalar_tensor_tensor(
                               bsl(x_tok, tc * 1024 + fg * 512, tc * 1024 + (fg + 1) * 512),
                               ps_t[bk][:, 0:512],
                               bsl(ffst, tc * 24 + 19, tc * 24 + 20),
                               bsl(x_tok, tc * 1024 + fg * 512, tc * 1024 + (fg + 1) * 512),
                               ALU.mult, ALU.add), "ffd")

            for p0 in range(0, GK, 2):
                emit_zq(p0, g0 + p0)
                emit_zq(p0 + 1, g0 + p0 + 1)
                emit_down_pair(p0)
            BA.free(z_tok)
            BA.free(wdbs[0]); BA.free(wdbs[1]); BA.free(zqf)
        for g0 in range(0, TCH, GK):
            emit_ffn_group(g0)
        BA.free(yq_c); FA.free(t5f); FA.free(mbf); FA.free(c1ff); FA.free(ffst)
        FA.free(stats)

        for si in range(2):
            DMA("sync",
                bass.AP(out.tensor, (si * L + t0) * D, [[D, 128], [128 * D, TCS], [1, D]]),
                bap(x_tok, si * TCS * 1024, [[1024, TCS], [1, 1024]]),
                [x_tok], [dbuf(out)], "store")

    pre_phase()
    for ti in range(NTILE):
        emit_tile(ti)
    OP("sync", [dram_bufs["out"]], [], lambda e: e.nop(), "fence")

    P.finalize()
    with nc.Block() as block:
        for eng in ENGS:
            def run_stream(e, eng=eng):
                for it in P.items[eng]:
                    for w in it["waits"]:
                        e.wait_ge(sems[w[1]], P.wait_value(w))
                    if it["drain"]:
                        e.drain()
                    inst = it["emit"](e)
                    if it["dma"] is not None:
                        inst.then_inc(sems[P.dma_seminfo[it["id"]][0]], 16)
                    elif it["inc"]:
                        inst.then_inc(sems[eng], 1)
            getattr(block, eng)(run_stream)
    st.close()
    return nc


def kernel(**inputs):
    x = np.ascontiguousarray(inputs["L0_attn_hidden_states"], dtype=np.float32)
    B, L, _ = x.shape
    TPER = 512
    NTILE = L // TPER

    consts = host_consts()
    feed = {
        "W_q": inputs["L0_attn_W_q"], "W_k": inputs["L0_attn_W_k"],
        "W_v": inputs["L0_attn_W_v"], "W_b": inputs["L0_attn_W_b"],
        "conv_w_q": inputs["L0_attn_conv_w_q"], "conv_w_k": inputs["L0_attn_conv_w_k"],
        "conv_w_v": inputs["L0_attn_conv_w_v"],
        "fir_short_w": np.ascontiguousarray(inputs["L0_attn_fir_short_w"]).reshape(H * DV, 3),
        "fir_long_w": np.ascontiguousarray(inputs["L0_attn_fir_long_w"]).reshape(H * DV, 31),
        "W_fuse1": inputs["L0_attn_W_fuse1"], "b_fuse1": inputs["L0_attn_b_fuse1"],
        "W_fuse2": inputs["L0_attn_W_fuse2"], "b_fuse2": inputs["L0_attn_b_fuse2"],
        "others_log_tau": inputs["L0_attn_others_log_tau"],
        "norm_w": inputs["L0_attn_norm_w"], "W_o": inputs["L0_attn_W_o"],
        "w_gate": inputs["L0_ffn_w_gate"], "w_up": inputs["L0_ffn_w_up"],
        "w_down": inputs["L0_ffn_w_down"],
    }
    feed = {k: np.ascontiguousarray(v, dtype=np.float32) for k, v in feed.items()}
    for k, v in consts.items():
        feed["c_" + k] = v.astype(NPBF)
    feed["c_onesb"] = np.ones((1, 128), np.float32).astype(NPBF)
    feed["c_onesf_r"] = np.ones((1, 128), np.float32)
    feed["c_onesf_c"] = np.ones((128, 1), np.float32)

    nc = build(TPER=TPER, NTILE=NTILE)
    n = 8
    in_maps = []
    for c in range(n):
        m = dict(feed)
        m["x"] = np.ascontiguousarray(x[2 * c:2 * c + 2])
        in_maps.append(m)
    res = run_bass_kernel_spmd(nc, in_maps, core_ids=list(range(n)))
    return np.concatenate([r["out"] for r in res.results], axis=0)
